# version 16
# baseline (speedup 1.0000x reference)
import numpy as np
import ml_dtypes
import concourse.bass as bass
import concourse.mybir as mybir
import concourse.bass_utils as _bass_utils
from concourse.bass_utils import run_bass_kernel_spmd

if not getattr(_bass_utils, "_ldwopt_patched", False):
    _orig_run_command = _bass_utils.run_command

    def _run_command_ldwopt(cmd, *a, **k):
        if isinstance(cmd, list):
            cmd = ["--enable-ldw-opt=true" if c == "--enable-ldw-opt=false" else c
                   for c in cmd]
        return _orig_run_command(cmd, *a, **k)

    _bass_utils.run_command = _run_command_ldwopt
    _bass_utils._ldwopt_patched = True

F32 = mybir.dt.float32
BF16 = mybir.dt.bfloat16
U8 = mybir.dt.uint8
AF = mybir.ActivationFunctionType
ALU = mybir.AluOpType
AX = mybir.AxisListType
NPBF = ml_dtypes.bfloat16

B, S, DM = 2, 4096, 2048
H, KVH, HD = 16, 4, 128
REP = H // KVH
T = 1024
NT = 8
NKV = 32
NPRE = NKV - NT
HID = 8192
NC_ = 64
EPS_LN = 1e-5
EPS_AT = 1e-6
FWHT_SCALE = float(1.0 / np.sqrt(np.float32(HID)))

KB = 1024


class Ctr:
    def __init__(self, handle, name):
        self.h = handle
        self.name = name
        self.n = 0


class Prog:
    ENGINES = ("sync", "gpsimd", "tensor", "vector", "scalar")

    def __init__(self):
        self.q = {e: [] for e in self.ENGINES}

    def em(self, eng, fn, wait=(), inc=None, inc_by=1):
        w = tuple((c.h, v) for (c, v) in wait)
        if inc is not None:
            inc.n += inc_by
        ih = None if inc is None else inc.h

        def run(e, fn=fn, w=w, ih=ih, ib=inc_by):
            for (s, v) in w:
                e.wait_ge(s, v)
            r = fn(e)
            if ih is not None:
                r.then_inc(ih, ib)
        self.q[eng].append(run)
        return None if inc is None else inc.n


AR = {}
def _slot(name, off, nbytes):
    AR[name] = (off, nbytes)
    return off + nbytes

_o = 0
_o = _slot("ident", _o, 256)
_o = _slot("h128", _o, 256)
_o = _slot("h64i2", _o, 256)
_o = _slot("maskt", _o, 1024)
_o = _slot("kvmask", _o, 128)
_o = _slot("cpool", _o, 32)
_o = _slot("stats", _o, 256)
_o = _slot("dent", _o, 128)
_o = ((_o + 127) // 128) * 128
_o = _slot("scrB", _o, 2048)
_o = _slot("scrC", _o, 2048)
_o = _slot("xn_tm", _o, 8192)
WORK = _o
_o = _slot("WORK", _o, 176 * KB)
_o = _slot("qftm", _o, 2048)
_o = _slot("fsc", _o, 2048)
_o = _slot("ckv", _o, 512)
_o = _slot("skv", _o, 512)
_o = _slot("sT", _o, 1024)
_o = _slot("gst3", _o, 6240)
_o = _slot("gbx", _o, 1040)
_o = _slot("gout", _o, 2080)
_o = _slot("astat", _o, 128)
ARENA_BYTES = _o
assert ARENA_BYTES <= 207 * KB + 896, ARENA_BYTES
AR["cosown"] = (WORK + 154 * KB, 2048)
AR["sinown"] = (WORK + 156 * KB, 2048)
AR["xnfm"] = (WORK + 158 * KB, 4096)

WA = {
    "wqkv": (0, 96 * KB),
    "qffm": (96 * KB, 32 * KB),
    "kffm": (128 * KB, 8 * KB),
    "vaug": (136 * KB, 8704),
    "saug": (136 * KB + 8704, 8704),
    "kftmc": (153 * KB, 1 * KB),
    "xstgA": (162 * KB, 8 * KB),
}
WB = {
    "cat": (0, 32 * KB),
    "stall": (32 * KB, 32 * KB),
}
WC = {
    "catfm": (32 * KB, 32 * KB),
    "wot": (64 * KB, 64 * KB),
    "xstg": (128 * KB, 16 * KB),
    "asb": (144 * KB, 16 * KB),
}
WD = {
    "xn3fm": (0, 32 * KB),
    "stg": (32 * KB, 16 * KB),
    "hid": (48 * KB, 128 * KB),
}
WE = {
    "w2": (0, 8 * KB),
    "ybuf": (8 * KB, 32 * KB),
    "ybf": (40 * KB, 4 * KB),
    "outstg": (48 * KB, 16 * KB),
    "x2stg": (64 * KB, 16 * KB),
}


def build_nc(stop_after=None, dumps=(), nkv_limit=None):
    nc = bass.Bass("TRN2", target_bir_lowering=False)

    xown_h = nc.dram_tensor("xown", [NT, 128, DM], F32, kind="ExternalInput").ap()
    cosown_h = nc.dram_tensor("cosown", [NT, 128, HD], BF16, kind="ExternalInput").ap()
    sinown_h = nc.dram_tensor("sinown", [NT, 128, HD], BF16, kind="ExternalInput").ap()
    gmask_h = nc.dram_tensor("gmask", [128, 4], F32, kind="ExternalInput").ap()
    ident_h = nc.dram_tensor("ident", [128, 128], BF16, kind="ExternalInput").ap()
    h128_h = nc.dram_tensor("h128", [128, 128], BF16, kind="ExternalInput").ap()
    h64i2_h = nc.dram_tensor("h64i2", [128, 128], BF16, kind="ExternalInput").ap()
    maskt_h = nc.dram_tensor("maskt", [128, 4, 128], BF16, kind="ExternalInput").ap()
    wqkv_h = nc.dram_tensor("wqkv_s", [16, 128, 3072], F32, kind="ExternalInput").ap()
    wo_h = nc.dram_tensor("wo_s", [16, 128, 2048], F32, kind="ExternalInput").ap()
    w13_h = nc.dram_tensor("w13_s", [NC_, 128, 2, 16, 128], F32, kind="ExternalInput").ap()
    w2_h = nc.dram_tensor("w2_s", [16, 128, NC_, 128], F32, kind="ExternalInput").ap()
    out_h = nc.dram_tensor("out", [NT, 128, DM], F32, kind="ExternalOutput").ap()
    x2sp_h = nc.dram_tensor("x2sp", [NT, 128, DM], F32, kind="Internal").ap()
    ccin_h = nc.dram_tensor("ccin", [128, 520], F32, kind="Internal").ap()
    ccout_h = nc.dram_tensor("ccout", [512, 520], F32, kind="Internal",
                             addr_space="Local").ap()

    dump_h = {}
    for dn, shape, dt in dumps:
        dump_h[dn] = nc.dram_tensor("dump_" + dn, shape, dt, kind="ExternalOutput").ap()

    P = Prog()

    from contextlib import ExitStack
    with ExitStack() as _stk:
        arena = _stk.enter_context(nc.sbuf_tensor("arena", [128, ARENA_BYTES], U8))
        P0 = _stk.enter_context(nc.psum_tensor("P0", [128, 1024], F32))
        P1 = _stk.enter_context(nc.psum_tensor("P1", [128, 1024], F32))
        P2 = _stk.enter_context(nc.psum_tensor("P2", [128, 1024], F32))
        P3 = _stk.enter_context(nc.psum_tensor("P3", [128, 1024], F32))
        def _sem(name):
            return _stk.enter_context(nc.semaphore(name))
        hT = _sem("sT_"); hV = _sem("sV_"); hS = _sem("sS_")
        hXK0 = _sem("cXK0"); hXK1 = _sem("cXK1")
        hCK0 = _sem("cCK0"); hCK1 = _sem("cCK1")
        hCON = _sem("cCON"); hWKV = _sem("cWKV"); hWQ = _sem("cWQ")
        hWO0 = _sem("cWO0"); hWO1 = _sem("cWO1")
        hW130 = _sem("cW130"); hW131 = _sem("cW131")
        hW20 = _sem("cW20"); hW21 = _sem("cW21")
        hXC0 = _sem("cXC0"); hXC1 = _sem("cXC1")
        hSP = _sem("cSP")
        hXD0 = _sem("cXD0"); hXD1 = _sem("cXD1")
        hXE0 = _sem("cXE0"); hXE1 = _sem("cXE1")
        hOUT = _sem("cOUT")
        block = _stk.enter_context(nc.Block())
        TT_ = Ctr(hT, "T"); VV = Ctr(hV, "V"); SS = Ctr(hS, "S")
        cXK = [Ctr(hXK0, "XK0"), Ctr(hXK1, "XK1")]
        cCK = [Ctr(hCK0, "CK0"), Ctr(hCK1, "CK1")]
        cCON = Ctr(hCON, "CON"); cWKV = Ctr(hWKV, "WKV"); cWQ = Ctr(hWQ, "WQ")
        cWO = [Ctr(hWO0, "WO0"), Ctr(hWO1, "WO1")]
        cW13 = [Ctr(hW130, "W130"), Ctr(hW131, "W131")]
        cW2 = [Ctr(hW20, "W20"), Ctr(hW21, "W21")]
        cXC = [Ctr(hXC0, "XC0"), Ctr(hXC1, "XC1")]
        cSP = Ctr(hSP, "SP")
        cXD = [Ctr(hXD0, "XD0"), Ctr(hXD1, "XD1")]
        cXE = [Ctr(hXE0, "XE0"), Ctr(hXE1, "XE1")]
        cOUT = Ctr(hOUT, "OUT")

        def vw(off, nbytes, dt, shape):
            ap = arena[:, off:off + nbytes].bitcast(dt)
            if len(shape) == 2:
                assert ap.shape[1] == shape[1], (ap.shape, shape)
                return ap
            pat = "p (" + " ".join(f"d{i}" for i in range(len(shape) - 1)) + ") -> p " + \
                  " ".join(f"d{i}" for i in range(len(shape) - 1))
            kw = {f"d{i}": shape[i + 1] for i in range(len(shape) - 1)}
            return ap.rearrange(pat, **kw)

        ident = vw(*AR["ident"], BF16, [128, 128])
        h128 = vw(*AR["h128"], BF16, [128, 128])
        h64i2 = vw(*AR["h64i2"], BF16, [128, 128])
        maskt = vw(*AR["maskt"], BF16, [128, 4, 128])
        gmask = vw(AR["kvmask"][0], 16, F32, [128, 4])
        cpool = vw(*AR["cpool"], F32, [128, 8])
        stats = vw(*AR["stats"], F32, [128, 64])
        dent = vw(*AR["dent"], F32, [128, 32])

        def SLT(k, c):
            col = (k * 2 + c) * 4
            return stats[:, col:col + 1]
        scrB = vw(*AR["scrB"], F32, [128, 512])
        scrC = vw(*AR["scrC"], F32, [128, 512])
        xn_tm = [vw(AR["xn_tm"][0] + p * 4096, 4096, BF16, [128, 2048]) for p in range(2)]
        xnfm = vw(AR["xnfm"][0], 4096, BF16, [128, 16, 128])
        cosown = vw(*AR["cosown"], BF16, [128, NT, HD])
        sinown = vw(*AR["sinown"], BF16, [128, NT, HD])
        qftm = [vw(AR["qftm"][0] + p * 1024, 1024, BF16, [128, 512]) for p in range(2)]
        fsc = [vw(AR["fsc"][0] + p * 1024, 1024, BF16, [128, 4, 128]) for p in range(2)]
        sTv = vw(*AR["sT"], BF16, [128, 4, 128])

        def wv(m, name, dt, shape, per=None, parity=2):
            off0, sz = m[name]
            if per is None:
                return vw(WORK + off0, sz, dt, shape)
            return [vw(WORK + off0 + p * per, per, dt, shape) for p in range(parity)]

        wqkv = wv(WA, "wqkv", BF16, [128, 16, 3072])
        qffm = wv(WA, "qffm", BF16, [128, 16, 1024])
        vaug = vw(WORK + WA["vaug"][0], 8320, BF16, [128, NT, 4, 130])
        saug = vw(WORK + WA["saug"][0], 8320, BF16, [128, NT, 4, 130])
        kffm = wv(WA, "kffm", BF16, [128, 4, 1024])
        kftmc = wv(WA, "kftmc", BF16, [128, 4, 128])
        xstgA = wv(WA, "xstgA", F32, [128, 2048])
        cat = wv(WB, "cat", BF16, [128, NT, 2048])
        gb = vw(*AR["gbx"], BF16, [128, 4, 130])
        gst = [vw(AR["gst3"][0] + p * 2080, 2080, F32, [128, 520]) for p in range(3)]
        gf32 = vw(*AR["gout"], F32, [128, 520])
        astat = vw(*AR["astat"], F32, [128, 32])
        kftmc2 = [kftmc, vw(AR["ckv"][0], 1024, BF16, [128, 4, 128])]
        stall = vw(WORK + WB["stall"][0], 32 * KB, BF16, [128, NT, 4, 4, 128])
        catfm = wv(WC, "catfm", BF16, [128, 16, 1024])
        wot = wv(WC, "wot", BF16, [128, 16, 2048])
        xstg = wv(WC, "xstg", F32, [128, 2048], per=8192)
        asb = wv(WC, "asb", F32, [128, 2048], per=8192)
        xn3fm = wv(WD, "xn3fm", BF16, [128, 16, 1024])
        w13 = wv(WD, "stg", BF16, [128, 2, 16, 128], per=8192)
        hid = wv(WD, "hid", BF16, [128, NC_, T])
        w2 = wv(WE, "w2", BF16, [128, 16, 128], per=4096)
        ybuf = wv(WE, "ybuf", BF16, [128, NT, 2048])
        ybfE = wv(WE, "ybf", BF16, [128, 1024], per=2048)
        outstg = wv(WE, "outstg", F32, [128, 2048], per=8192)
        x2stgE = wv(WE, "x2stg", F32, [128, 2048], per=8192)

        DUMPVIEWS = {
            "kffm": kffm, "vaug": vaug, "saug": saug, "qffm": qffm, "gb": gb,
            "cat": cat, "xn3fm": xn3fm, "hid": hid, "ybuf": ybuf, "catfm": catfm,
            "stats": stats, "sT": sTv, "stall": stall,
        }

        em = P.em

        em("sync", lambda e: e.dma_start(out=ident, in_=ident_h), inc=cCON, inc_by=16)
        em("sync", lambda e: e.dma_start(out=h128, in_=h128_h), inc=cCON, inc_by=16)
        em("sync", lambda e: e.dma_start(out=h64i2, in_=h64i2_h), inc=cCON, inc_by=16)
        em("sync", lambda e: e.dma_start(out=maskt, in_=maskt_h), inc=cCON, inc_by=16)
        em("sync", lambda e: e.dma_start(out=gmask, in_=gmask_h), inc=cCON, inc_by=16)
        for i in range(NT):
            em("sync", lambda e, i=i: e.dma_start(out=cosown[:, i, :], in_=cosown_h[i]), inc=cCON, inc_by=16)
            em("sync", lambda e, i=i: e.dma_start(out=sinown[:, i, :], in_=sinown_h[i]), inc=cCON, inc_by=16)
        n_con = cCON.n

        for col, val in ((0, EPS_LN), (1, EPS_AT), (2, FWHT_SCALE),
                         (3, 1.0 / DM), (4, -1.0 / DM)):
            em("vector", lambda e, c=col, v=val: e.memset(cpool[:, c:c + 1], v), inc=VV)
        em("vector", lambda e: e.memset(vaug[:, :, :, 128:129], 1.0), inc=VV)
        em("vector", lambda e: e.memset(vaug[:, :, :, 129:130], 0.0), inc=VV)
        em("vector", lambda e: e.memset(gb[:, :, 129:130], 0.0), inc=VV)
        em("vector", lambda e: e.memset(saug[:, 0, :, :], 0.0), inc=VV)
        v_init = VV.n

        for d in range(16):
            em("gpsimd", lambda e, d=d: e.dma_start(out=wqkv[:, d, :], in_=wqkv_h[d]), inc=cWKV, inc_by=16)
        n_wkv = cWKV.n

        def ln_stats(src, par, trash_bf, wait_src):
            em("vector", lambda e, s=src, p=par: e.reduce_sum(
                SLT(0, p), s, axis=AX.X), wait=wait_src, inc=VV)
            v_sum = VV.n
            em("scalar", lambda e, p=par: e.activation(
                SLT(1, p), SLT(0, p), AF.Identity, scale=cpool[:, 4:5]),
                wait=[(VV, v_sum), (VV, v_init)], inc=SS)
            s_negm = SS.n
            em("scalar", lambda e, s=src, t=trash_bf, p=par: e.activation(
                t, s, AF.Square, bias=SLT(1, p), accum_out=SLT(2, p)),
                wait=wait_src + [(SS, s_negm)], inc=SS)
            s_var = SS.n
            em("scalar", lambda e, p=par: e.activation(
                SLT(3, p), SLT(2, p), AF.Sqrt, scale=cpool[:, 3:4], bias=cpool[:, 0:1]),
                wait=[(SS, s_var)], inc=SS)
            s_sd = SS.n
            em("vector", lambda e, p=par: e.reciprocal(
                SLT(4, p), SLT(3, p)), wait=[(SS, s_sd)], inc=VV)
            v_r = VV.n
            em("scalar", lambda e, p=par: e.activation(
                SLT(5, p), SLT(1, p), AF.Identity, scale=SLT(4, p)),
                wait=[(VV, v_r)], inc=SS)
            s_nmr = SS.n
            return s_nmr, SLT(4, par), SLT(5, par)

        def transpose4(ps_half, in_tile_fn, dst_ap, wait_in, evac_free_wait, evac_wait=()):
            for k in range(4):
                w = list(wait_in) + (list(evac_free_wait) if k == 0 else [])
                em("tensor", lambda e, k=k, f=in_tile_fn, ps=ps_half: e.matmul(
                    ps[:, k * 128:(k + 1) * 128], f(k), ident, start=True, stop=True),
                    wait=w, inc=(TT_ if k == 3 else None))
            t_mm = TT_.n
            ps4 = ps_half.rearrange("p (a b) -> p a b", a=4)
            em("scalar", lambda e, ps=ps4, d=dst_ap: e.activation(d, ps, AF.Identity),
               wait=[(TT_, t_mm)] + list(evac_wait), inc=SS)
            return t_mm, SS.n

        state_ps = P3
        st_off = [(0, 0), (0, 192), (512, 0), (512, 192)]
        QBANKS = (P0[:, 0:512], P0[:, 512:1024], P1[:, 0:512], P1[:, 512:1024])

        cXA, cSO, cGI = cXK[0], cXK[1], cCK[0]
        cCC = cWQ
        psT_free = [(SS, 0), (SS, 0)]
        psQ_free = [(VV, 0)] * 4
        xn_free = [(TT_, 0), (TT_, 0)]
        xstg_free = (SS, 0)
        t_state = {}
        s_snap = {}
        psTpar = 0
        qftm_guard = [(TT_, 0), (TT_, 0)]
        scrB3 = scrB.rearrange("p (a b) -> p a b", a=4)
        scrC3 = scrC.rearrange("p (a b) -> p a b", a=4)

        t_kv_last = 0
        p1_free = []
        a1_pend = None

        def a1_flush(t, v_kf, s_ve):
            nonlocal psTpar
            pp = psTpar; psTpar ^= 1
            ps_half = (P2[:, 0:512], P2[:, 512:1024])[pp]
            _, s_kfe = transpose4(ps_half, (lambda k, t=t: kftmc2[t % 2][:, k, :]),
                                  kffm[:, :, t * 128:(t + 1) * 128],
                                  wait_in=[(VV, v_kf)], evac_free_wait=[psT_free[pp]])
            psT_free[pp] = (SS, s_kfe)
            if t == 0:
                s_snap[t] = 0
            else:
                wsn = [(TT_, t_state.get(t - 1, 0))]
                for halfsel in range(2):
                    srcp = state_ps[:, halfsel * 512:halfsel * 512 + 384].rearrange(
                        "p (g x) -> p g x", g=2)[:, :, 0:130]
                    dst = saug[:, t, halfsel * 2:halfsel * 2 + 2, :]
                    em("scalar", lambda e, s=srcp, d=dst: e.activation(d, s, AF.Identity),
                       wait=(wsn if halfsel == 0 else []), inc=(SS if halfsel == 1 else None))
                s_snap[t] = SS.n
            for g in range(4):
                half, off = st_off[g]
                stf = (t == 0 and off == 0)
                em("tensor", lambda e, g=g, h=half, o=off, stf=stf, t=t: e.matmul(
                    state_ps[:, h + o:h + o + 130], kftmc2[t % 2][:, g, :],
                    vaug[:, t, g, 0:130], start=stf, stop=(t == NT - 1)),
                    wait=([(VV, v_kf), (SS, s_ve), (SS, s_snap[t])] if g == 0 else []),
                    inc=(TT_ if g == 3 else None))
            t_state[t] = TT_.n

        for t in range(NT):
            par = t % 2
            em("sync", lambda e, t=t: e.dma_start(out=xstgA, in_=xown_h[t]),
               wait=[xstg_free], inc=cXA, inc_by=16)
            w_x = [(cXA, cXA.n)]
            s_nmr, r_ap, nmr_ap = ln_stats(xstgA, par, xn_tm[par], w_x + [xn_free[par]])
            em("scalar", lambda e, p=par, r=r_ap, m=nmr_ap: e.activation(
                xn_tm[p], xstgA, AF.Identity, bias=m, scale=r), wait=[(SS, s_nmr)], inc=SS)
            s_norm = SS.n
            xstg_free = (SS, s_norm)
            em("scalar", lambda e, t=t, m=nmr_ap: e.activation(
                astat[:, t:t + 1], m, AF.Identity), wait=[(SS, s_nmr)], inc=SS)
            em("scalar", lambda e, t=t, r=r_ap: e.activation(
                astat[:, 16 + t:17 + t], r, AF.Identity), inc=SS)
            s_ast = SS.n
            s_fme = [0] * 4
            for grp in range(4):
                pp = psTpar; psTpar ^= 1
                ps_half = (P2[:, 0:512], P2[:, 512:1024])[pp]
                tf = (lambda k, g=grp, p=par: xn_tm[p][:, (g * 4 + k) * 128:(g * 4 + k + 1) * 128])
                _, s_fme[grp] = transpose4(
                    ps_half, tf, xnfm[:, grp * 4:grp * 4 + 4, :],
                    wait_in=[(SS, s_norm)] + ([(cCON, n_con)] if t == 0 else []),
                    evac_free_wait=[psT_free[pp]],
                    evac_wait=[(TT_, t_kv_last)])
                psT_free[pp] = (SS, s_fme[grp])
            xn_free[par] = (TT_, TT_.n)
            for d in range(16):
                w = [(SS, s_fme[d // 4])]
                if d == 0:
                    w += p1_free
                    if t == 0:
                        w.append((cWKV, n_wkv))
                em("tensor", lambda e, d=d: e.matmul(
                    P1[:, 0:512], xnfm[:, d, :], wqkv[:, d, 2048:2560],
                    start=(d == 0), stop=(d == 15)), wait=w)
                em("tensor", lambda e, d=d: e.matmul(
                    P1[:, 512:1024], xnfm[:, d, :], wqkv[:, d, 2560:3072],
                    start=(d == 0), stop=(d == 15)), inc=(TT_ if d == 15 else None))
            t_kv = TT_.n
            t_kv_last = t_kv
            if a1_pend is not None:
                a1_flush(*a1_pend)
                a1_pend = None
            em("scalar", lambda e, t=t: e.activation(
                vaug[:, t, :, 0:128], P1[:, 512:1024].rearrange("p (a b) -> p a b", a=4),
                AF.Identity), wait=[(TT_, t_kv)] + ([(VV, v_init)] if t == 0 else []), inc=SS)
            s_ve = SS.n
            psK3 = P1[:, 0:512].rearrange("p (a b) -> p a b", a=4)
            cb = cosown[:, t, :].unsqueeze(1).broadcast_to([128, 4, HD])
            sb = sinown[:, t, :].unsqueeze(1).broadcast_to([128, 4, HD])
            em("vector", lambda e, s=psK3: e.tensor_scalar(
                scrB3[:, :, 0:64], s[:, :, 64:128], -1.0, None, ALU.mult),
                wait=[(TT_, t_kv)], inc=VV)
            em("vector", lambda e, s=psK3: e.tensor_copy(scrB3[:, :, 64:128], s[:, :, 0:64]), inc=VV)
            em("vector", lambda e, s=psK3, c=cb: e.tensor_tensor(scrC3, s, c, ALU.mult), inc=VV)
            v_kps = VV.n
            em("vector", lambda e, s=sb: e.tensor_tensor(scrB3, scrB3, s, ALU.mult), inc=VV)
            em("vector", lambda e: e.tensor_tensor(scrC, scrC, scrB, ALU.add), inc=VV)
            em("vector", lambda e: e.tensor_scalar(scrB, scrC, 0.0, None, ALU.min), inc=VV)
            v_min = VV.n
            em("scalar", lambda e: e.activation(scrB, scrB, AF.Exp), wait=[(VV, v_min)], inc=SS)
            s_exp = SS.n
            em("vector", lambda e, t=t: e.scalar_tensor_tensor(
                kftmc2[t % 2], scrC3, 0.0, scrB3, ALU.max, ALU.add),
                wait=[(SS, s_exp), (TT_, t_state.get(t - 2, 0))], inc=VV)
            v_kf = VV.n
            p1_free = [(VV, v_kps), (SS, s_ve)]
            a1_pend = (t, v_kf, s_ve)
        a1_flush(*a1_pend)
        a1_pend = None
        a1_s_ast = SS.n

        gf4 = gf32.rearrange("p (g x) -> p g x", g=4)
        for halfsel in range(2):
            srcp = state_ps[:, halfsel * 512:halfsel * 512 + 384].rearrange(
                "p (g x) -> p g x", g=2)[:, :, 0:130]
            em("scalar", lambda e, s=srcp, hs=halfsel: e.activation(
                gf4[:, hs * 2:hs * 2 + 2, :], s, AF.Identity),
                wait=([(TT_, t_state[NT - 1])] if halfsel == 0 else []),
                inc=(SS if halfsel == 1 else None))
        s_stot = SS.n
        em("sync", lambda e: e.dma_start(out=ccin_h, in_=gf32),
           wait=[(SS, s_stot)], inc=cSO, inc_by=16)
        em("gpsimd", lambda e: e.collective_compute(
            "AllGather", mybir.AluOpType.bypass, ins=[ccin_h], outs=[ccout_h],
            replica_groups=[[0, 1, 2, 3], [4, 5, 6, 7]]),
            wait=[(cSO, cSO.n)], inc=cCC)
        ccv = ccout_h.rearrange("(r p) f -> p r f", p=128)
        for k in range(3):
            em("gpsimd", lambda e, k=k: e.dma_start(out=gst[k], in_=ccv[:, k, :]),
               wait=([(cCC, 1)] if k == 0 else []), inc=cGI, inc_by=16)
        n_gi = cGI.n

        A1_t, A1_v = TT_.n, VV.n
        t_q_last = t_kv_last
        xn_free = [(TT_, A1_t), (TT_, A1_t)]
        xstg_free = xstg_free
        psQ_free = [(VV, A1_v), (VV, A1_v)] + p1_free[:1] + [p1_free[1]]
        for t in range(NT):
            par = t % 2
            em("sync", lambda e, t=t: e.dma_start(out=xstgA, in_=xown_h[t]),
               wait=[xstg_free], inc=cXA, inc_by=16)
            w_x = [(cXA, cXA.n)]
            em("scalar", lambda e, p=par, t=t: e.activation(
                xn_tm[p], xstgA, AF.Identity, bias=astat[:, t:t + 1],
                scale=astat[:, 16 + t:17 + t]),
                wait=w_x + [xn_free[par], (SS, a1_s_ast)], inc=SS)
            s_norm = SS.n
            xstg_free = (SS, s_norm)
            s_fme = [0] * 4
            for grp in range(4):
                pp = psTpar; psTpar ^= 1
                ps_half = (P2[:, 0:512], P2[:, 512:1024])[pp]
                tf = (lambda k, g=grp, p=par: xn_tm[p][:, (g * 4 + k) * 128:(g * 4 + k + 1) * 128])
                _, s_fme[grp] = transpose4(
                    ps_half, tf, xnfm[:, grp * 4:grp * 4 + 4, :],
                    wait_in=[(SS, s_norm)],
                    evac_free_wait=[psT_free[pp]],
                    evac_wait=[(TT_, t_q_last)])
                psT_free[pp] = (SS, s_fme[grp])
            xn_free[par] = (TT_, TT_.n)
            for d in range(16):
                w = [(SS, s_fme[d // 4])]
                if d == 0:
                    w += psQ_free
                for g in range(4):
                    em("tensor", lambda e, d=d, g=g: e.matmul(
                        QBANKS[g], xnfm[:, d, :], wqkv[:, d, g * 512:(g + 1) * 512],
                        start=(d == 0), stop=(d == 15)),
                        wait=(w if g == 0 else []), inc=(TT_ if (d == 15 and g == 3) else None))
            t_q = TT_.n
            t_q_last = t_q
            cb = cosown[:, t, :].unsqueeze(1).broadcast_to([128, 4, HD])
            sb = sinown[:, t, :].unsqueeze(1).broadcast_to([128, 4, HD])
            for g in range(4):
                ps3 = QBANKS[g].rearrange("p (a b) -> p a b", a=4)
                em("vector", lambda e, s=ps3: e.tensor_scalar(
                    scrB3[:, :, 0:64], s[:, :, 64:128], -1.0, None, ALU.mult),
                    wait=[(TT_, t_q)], inc=VV)
                em("vector", lambda e, s=ps3: e.tensor_copy(scrB3[:, :, 64:128], s[:, :, 0:64]), inc=VV)
                em("vector", lambda e, s=ps3, c=cb: e.tensor_tensor(scrC3, s, c, ALU.mult), inc=VV)
                psQ_free[g] = (VV, VV.n)
                em("vector", lambda e, s=sb: e.tensor_tensor(scrB3, scrB3, s, ALU.mult), inc=VV)
                em("vector", lambda e: e.tensor_tensor(scrC, scrC, scrB, ALU.add), inc=VV)
                em("vector", lambda e: e.tensor_scalar(scrB, scrC, 0.0, None, ALU.min), inc=VV)
                v_min = VV.n
                em("scalar", lambda e: e.activation(scrB, scrB, AF.Exp), wait=[(VV, v_min)], inc=SS)
                s_exp = SS.n
                em("vector", lambda e, g=g: e.scalar_tensor_tensor(
                    qftm[g % 2], scrC, 0.0, scrB, ALU.max, ALU.add),
                    wait=[(SS, s_exp), qftm_guard[g % 2]], inc=VV)
                v_qf = VV.n
                pp = psTpar; psTpar ^= 1
                ps_half = (P2[:, 0:512], P2[:, 512:1024])[pp]
                tfq = (lambda k, g=g: qftm[g % 2][:, k * 128:(k + 1) * 128])
                tmm_qf, s_qt = transpose4(ps_half, tfq,
                                          qffm[:, g * 4:g * 4 + 4, t * 128:(t + 1) * 128],
                                          wait_in=[(VV, v_qf)],
                                          evac_free_wait=[psT_free[pp]])
                qftm_guard[g % 2] = (TT_, tmm_qf)
                psT_free[pp] = (SS, s_qt)

        em("vector", lambda e: e.tensor_scalar(gst[0], gst[0], gmask[:, 0:1], None, ALU.mult),
           wait=[(cGI, n_gi), (cCON, n_con)], inc=VV)
        em("vector", lambda e: e.scalar_tensor_tensor(
            gst[0], gst[1], gmask[:, 1:2], gst[0], ALU.mult, ALU.add), wait=[(VV, VV.n)], inc=VV)
        em("vector", lambda e: e.scalar_tensor_tensor(
            gst[0], gst[2], gmask[:, 2:3], gst[0], ALU.mult, ALU.add), wait=[(VV, VV.n)], inc=VV)
        em("vector", lambda e: e.tensor_copy(
            gb[:, :, 0:130], gst[0].rearrange("p (g x) -> p g x", g=4)),
            wait=[(VV, VV.n)], inc=VV)
        v_gb = VV.n

        if stop_after == "A":
            _emit_dumps(P, nc, dumps, dump_h, DUMPVIEWS, TT_, VV, SS, cOUT)
            _finalize(nc, block, P)
            return nc

        A_t, A_v, A_s = TT_.n, VV.n, SS.n
        psB_free = [(SS, s_stot), (SS, s_stot)]

        for t in range(NT):
            for g in range(4):
                pp = (t * 4 + g) % 2
                ps_half = (P3[:, 0:512], P3[:, 512:1024])[pp]
                em("tensor", lambda e, g=g, t=t, ps=ps_half: e.matmul(
                    ps, kffm[:, g, t * 128:(t + 1) * 128],
                    qffm[:, g * 4:g * 4 + 4, t * 128:(t + 1) * 128],
                    start=True, stop=True), wait=[psB_free[pp]], inc=TT_)
                t_sc = TT_.n
                em("vector", lambda e, g=g, t=t, ps=ps_half: e.tensor_tensor(
                    stall[:, t, g, :, :], ps.rearrange("p (a b) -> p a b", a=4),
                    maskt, ALU.mult),
                    wait=[(TT_, t_sc)] + ([(cCON, n_con)] if (t == 0 and g == 0) else []),
                    inc=VV)
                psB_free[pp] = (VV, VV.n)

        psU_free = [psB_free[0], psB_free[1], (VV, A_v), (VV, A_v)]
        for t in range(NT):
            for g in range(4):
                u = (t * 4 + g) % 2
                PU = (P3, P0)[u]
                dof = u * 2
                for h in range(4):
                    half, off = st_off[h]
                    w0 = ([psU_free[2 * u], (VV, v_gb)] if h == 0 else
                          ([psU_free[2 * u + 1]] if h == 2 else []))
                    em("tensor", lambda e, h=h, hf=half, o=off, g=g, t=t, PU=PU: e.matmul(
                        PU[:, hf + o:hf + o + 130], stall[:, t, g, h, :], vaug[:, t, g, 0:130],
                        start=(o == 0), stop=False), wait=w0)
                    em("tensor", lambda e, h=h, hf=half, o=off, g=g, t=t, PU=PU: e.matmul(
                        PU[:, hf + o:hf + o + 130], qffm[:, g * 4 + h, t * 128:(t + 1) * 128],
                        saug[:, t, g, 0:130], start=False, stop=False))
                    em("tensor", lambda e, h=h, hf=half, o=off, g=g, t=t, PU=PU: e.matmul(
                        PU[:, hf + o:hf + o + 130], qffm[:, g * 4 + h, t * 128:(t + 1) * 128],
                        gb[:, g, 0:130], start=False, stop=True), inc=(TT_ if h == 3 else None))
                t_nm = TT_.n
                for hh in range(2):
                    srcd = PU[:, hh * 512:hh * 512 + 384].rearrange(
                        "p (g x) -> p g x", g=2)[:, :, 128:129].rearrange("p a b -> p (a b)")
                    em("scalar", lambda e, hh=hh, s=srcd, do=dof: e.activation(
                        dent[:, (do + hh) * 4:(do + hh) * 4 + 2], s, AF.Identity,
                        bias=cpool[:, 1:2]),
                        wait=([(TT_, t_nm)] if hh == 0 else []), inc=SS)
                s_den = SS.n
                em("vector", lambda e, do=dof: e.reciprocal(
                    dent[:, 16 + do * 4:18 + do * 4], dent[:, do * 4:do * 4 + 2]),
                   wait=[(SS, s_den)], inc=VV)
                em("vector", lambda e, do=dof: e.reciprocal(
                    dent[:, 20 + do * 4:22 + do * 4], dent[:, do * 4 + 4:do * 4 + 6]), inc=VV)
                v_rec = VV.n
                for h in range(4):
                    half, off = st_off[h]
                    rcol = (16 + h if h < 2 else 18 + h) + dof * 4
                    em("vector", lambda e, h=h, hf=half, o=off, g=g, t=t, rc=rcol, PU=PU: e.tensor_scalar(
                        cat[:, t, (g * 4 + h) * 128:(g * 4 + h + 1) * 128],
                        PU[:, hf + o:hf + o + 128], dent[:, rc:rc + 1], None, ALU.mult),
                        wait=([(VV, v_rec)] if h == 0 else []), inc=(VV if h == 3 else None))
                psU_free[2 * u] = (VV, VV.n)
                psU_free[2 * u + 1] = (VV, VV.n)

        v_scan_end = VV.n
        for t in range(NT):
            for grp in range(4):
                pp = psTpar; psTpar ^= 1
                ps_half = (P2[:, 0:512], P2[:, 512:1024])[pp]
                tf = (lambda k, t=t, gg=grp: cat[:, t, (gg * 4 + k) * 128:(gg * 4 + k + 1) * 128])
                _, s_cfm = transpose4(ps_half, tf,
                                      catfm[:, grp * 4:grp * 4 + 4, t * 128:(t + 1) * 128],
                                      wait_in=([(VV, v_scan_end)] if (t == 0 and grp == 0) else []),
                                      evac_free_wait=[psT_free[pp]])
                psT_free[pp] = (SS, s_cfm)

        if stop_after == "B":
            _emit_dumps(P, nc, dumps, dump_h, DUMPVIEWS, TT_, VV, SS, cOUT)
            _finalize(nc, block, P)
            return nc

        B_t, B_v, B_s = TT_.n, VV.n, SS.n
        for f in range(16):
            w = [(TT_, A_t)] if f == 0 else ([(TT_, B_t)] if f == 8 else [])
            em("gpsimd", lambda e, f=f: e.dma_start(out=wot[:, f, :], in_=wo_h[f]),
               wait=w, inc=cWO[0], inc_by=16)
        n_wot = cWO[0].n

        xc_free = [[(TT_, B_t)], [(TT_, B_t)]]
        asb_free = [(SS, 0), (SS, 0)]
        psWo_free = [psU_free[3], (VV, A_v)]
        c_psT_free = [psT_free[0], psT_free[1]]
        s_fm3 = 0
        xp_pend = []

        def emit_xp(t, par, s_norm3):
            nonlocal psTpar, s_fm3
            for grp in range(4):
                pp = psTpar; psTpar ^= 1
                ps_half = (P2[:, 0:512], P2[:, 512:1024])[pp]
                tf = (lambda k, g=grp, p=par: xn_tm[p][:, (g * 4 + k) * 128:(g * 4 + k + 1) * 128])
                tmm, sf = transpose4(ps_half, tf,
                                     xn3fm[:, grp * 4:grp * 4 + 4, t * 128:(t + 1) * 128],
                                     wait_in=[(SS, s_norm3)], evac_free_wait=[c_psT_free[pp]])
                c_psT_free[pp] = (SS, sf)
                s_fm3 = sf
            xn_free[par] = (TT_, TT_.n)

        for t in range(NT):
            par = t % 2
            em("sync", lambda e, t=t, p=par: e.dma_start(out=xstg[p], in_=xown_h[t]),
               wait=list(xc_free[par]), inc=cXC[par], inc_by=16)
            w_x = [(cXC[par], cXC[par].n)]
            s_evac = 0
            for coh in range(2):
                psW = (P0, P1)[coh]
                for f in range(16):
                    w0 = []
                    if f == 0:
                        w0 = [psWo_free[coh], (cWO[0], n_wot)]
                        if t == 0 and coh == 0:
                            w0.append((SS, B_s))
                    for nh in range(2):
                        em("tensor", lambda e, f=f, nh=nh, t=t, coh=coh, ps=psW: e.matmul(
                            ps[:, nh * 512:(nh + 1) * 512],
                            catfm[:, f, t * 128:(t + 1) * 128],
                            wot[:, f, coh * 1024 + nh * 512:coh * 1024 + (nh + 1) * 512],
                            start=(f == 0), stop=(f == 15)),
                            wait=(w0 if nh == 0 else []),
                            inc=(TT_ if (f == 15 and nh == 1) else None))
                t_wo = TT_.n
                wev = [(TT_, t_wo)]
                if coh == 0:
                    wev.append(asb_free[par])
                em("scalar", lambda e, coh=coh, p=par, ps=psW: e.activation(
                    asb[p][:, coh * 1024:(coh + 1) * 1024], ps[:, 0:1024], AF.Identity,
                    accum_out=SLT(6, coh)), wait=wev, inc=SS)
                psWo_free[coh] = (SS, SS.n)
            s_evac = SS.n
            if xp_pend and xp_pend[0][0] < t:
                emit_xp(*xp_pend.pop(0))
            em("scalar", lambda e, p=par: e.activation(
                SLT(0, p), SLT(6, 0), AF.Identity, bias=SLT(6, 1)),
                wait=[(SS, s_evac)], inc=SS)
            s_sum = SS.n
            em("scalar", lambda e, p=par: e.activation(
                SLT(1, p), SLT(0, p), AF.Identity, scale=cpool[:, 4:5]),
                wait=[(SS, s_sum)], inc=SS)
            s_negm = SS.n
            em("scalar", lambda e, p=par: e.activation(
                xn_tm[p], asb[p], AF.Square, bias=SLT(1, p), accum_out=SLT(2, p)),
                wait=[(SS, s_negm), xn_free[par]], inc=SS)
            s_var = SS.n
            em("scalar", lambda e, p=par: e.activation(
                SLT(3, p), SLT(2, p), AF.Sqrt, scale=cpool[:, 3:4], bias=cpool[:, 0:1]),
                wait=[(SS, s_var)], inc=SS)
            s_sd = SS.n
            em("vector", lambda e, p=par: e.reciprocal(SLT(4, p), SLT(3, p)),
               wait=[(SS, s_sd)], inc=VV)
            v_r = VV.n
            em("scalar", lambda e, p=par: e.activation(
                SLT(5, p), SLT(1, p), AF.Identity, scale=SLT(4, p)),
                wait=[(VV, v_r)], inc=SS)
            s_nmr = SS.n
            for k in range(4):
                scr = (scrB, scrC)[k % 2]
                em("scalar", lambda e, t=t, k=k, s=scr, p=par: e.activation(
                    s, asb[p][:, k * 512:(k + 1) * 512], AF.Identity,
                    bias=SLT(5, p), scale=SLT(4, p)),
                    wait=[(SS, s_nmr)] if k == 0 else [(VV, VV.n)], inc=SS)
                em("vector", lambda e, k=k, p=par, s=scr: e.tensor_tensor(
                    xstg[p][:, k * 512:(k + 1) * 512], xstg[p][:, k * 512:(k + 1) * 512],
                    s, ALU.add), wait=[(SS, SS.n)] + (w_x if k == 0 else []), inc=VV)
            v_x2 = VV.n
            asb_free[par] = (SS, SS.n)
            em("sync", lambda e, t=t, p=par: e.dma_start(out=x2sp_h[t], in_=xstg[p]),
               wait=[(VV, v_x2)], inc=cSP, inc_by=16)
            s_nmr3, r3, nmr3 = ln_stats(xstg[par], par, xn_tm[par], [(VV, v_x2)])
            em("scalar", lambda e, p=par, r=r3, m=nmr3: e.activation(
                xn_tm[p], xstg[p], AF.Identity, bias=m, scale=r), wait=[(SS, s_nmr3)], inc=SS)
            s_norm3 = SS.n
            xc_free[par] = [(cSP, cSP.n), (SS, s_norm3)]
            xp_pend.append((t, par, s_norm3))
        for t, par, s_norm3 in xp_pend:
            emit_xp(t, par, s_norm3)
        n_sp = cSP.n
        d_psT_free = c_psT_free

        if stop_after == "C":
            _emit_dumps(P, nc, dumps, dump_h, DUMPVIEWS, TT_, VV, SS, cOUT)
            _finalize(nc, block, P)
            return nc

        s_d0, v_d0 = SS.n, VV.n
        w13_free = [(SS, s_d0), (SS, s_d0)]
        psAB_free = [(SS, s_d0), (SS, s_d0)]
        v_hid = 0
        for c in range(NC_):
            par = c % 2
            wfill = [w13_free[par]] + ([(VV, v_d0)] if c < 2 else [])
            em("gpsimd", lambda e, c=c, p=par: e.dma_start(out=w13[p], in_=w13_h[c]),
               wait=wfill, inc=cW13[par], inc_by=16)
            psA = (P0, P2)[par]
            psB = (P1, P3)[par]
            w0 = [(cW13[par], cW13[par].n), (SS, s_fm3), psAB_free[par]]
            for w13i in range(2):
                ps = (psA, psB)[w13i]
                for d in range(16):
                    for nh in range(2):
                        em("tensor", lambda e, w=w13i, d=d, nh=nh, p=par, ps=ps: e.matmul(
                            ps[:, nh * 512:(nh + 1) * 512], w13[p][:, w, d, :],
                            xn3fm[:, d, nh * 512:(nh + 1) * 512], start=(d == 0), stop=(d == 15)),
                            wait=(w0 if (w13i == 0 and d == 0 and nh == 0) else []),
                            inc=(TT_ if (w13i == 1 and d == 15 and nh == 1) else None))
            t_13 = TT_.n
            w13_free[par] = (TT_, t_13)
            for nh in range(2):
                scr = (scrB, scrC)[nh]
                em("scalar", lambda e, s=scr, nh=nh, ps=psA: e.activation(
                    s, ps[:, nh * 512:(nh + 1) * 512], AF.Silu),
                    wait=[(TT_, t_13)] if nh == 0 else [], inc=(SS if nh == 1 else None))
            s_si = SS.n
            for nh in range(2):
                scr = (scrB, scrC)[nh]
                em("vector", lambda e, c=c, nh=nh, s=scr, ps=psB: e.tensor_tensor(
                    hid[:, c, nh * 512:(nh + 1) * 512], s, ps[:, nh * 512:(nh + 1) * 512],
                    ALU.mult), wait=[(SS, s_si)] if nh == 0 else [], inc=(VV if nh == 1 else None))
            v_hid = VV.n
            psAB_free[par] = (VV, v_hid)
        D2_s0 = SS.n
        PS1Q = [P0[:, 0:512], P0[:, 512:1024], P2[:, 0:512], P2[:, 512:1024]]
        PS2Q = [P1[:, 0:512], P1[:, 512:1024], P3[:, 0:512], P3[:, 512:1024]]
        e1q = [qftm[0], qftm[1],
               vw(AR["scrB"][0], 1024, BF16, [128, 512]),
               vw(AR["scrB"][0] + 1024, 1024, BF16, [128, 512])]
        fscq = [fsc[0], fsc[1],
                vw(AR["scrC"][0], 1024, BF16, [128, 4, 128]),
                vw(AR["scrC"][0] + 1024, 1024, BF16, [128, 4, 128])]
        psF1_free = [(SS, D2_s0), (SS, D2_s0), (SS, D2_s0), (SS, D2_s0)]
        psF2_free = [(VV, v_hid), (VV, v_hid), (VV, v_hid), (VV, v_hid)]
        e1_free = [(TT_, 0), (TT_, 0), (VV, v_hid), (VV, v_hid)]
        fsc_free = [(TT_, 0), (TT_, 0), (VV, v_hid), (VV, v_hid)]

        def emit_gather(grp):
            q = grp % 4
            em("vector", lambda e, g8=grp * 8, q=q: e.tensor_copy(
                fscq[q].rearrange("p q (c t) -> p q c t", c=64),
                hid[:, :, g8:g8 + 8].rearrange("p c (q t) -> p q c t", q=4)),
                wait=[fsc_free[q]], inc=VV)
            return VV.n

        t_f1s = {}

        def emit_A(grp, v_g):
            q = grp % 4
            psF1 = PS1Q[q]
            for k in range(4):
                w = [(VV, v_g), psF1_free[q]] if k == 0 else []
                em("tensor", lambda e, k=k, q=q, ps=psF1: e.matmul(
                    ps[:, k * 128:(k + 1) * 128], fscq[q][:, k, :], h128,
                    start=True, stop=True), wait=w, inc=(TT_ if k == 3 else None))
            t_f1s[grp] = TT_.n
            fsc_free[q] = (TT_, TT_.n)

        s_evacs = {}

        def emit_evac(grp):
            q = grp % 4
            em("scalar", lambda e, d=e1q[q], ps=PS1Q[q]: e.activation(
                d, ps, AF.Identity, scale=cpool[:, 2:3]),
                wait=[(TT_, t_f1s[grp]), e1_free[q]] + ([(VV, v_init)] if grp == 0 else []), inc=SS)
            s_evacs[grp] = SS.n
            psF1_free[q] = (SS, SS.n)

        v_gs = {}
        for g0 in range(4):
            v_gs[g0] = emit_gather(g0)
            emit_A(g0, v_gs[g0])
        emit_evac(0)
        emit_evac(1)
        for grp in range(128):
            q = grp % 4
            psF2 = PS2Q[q]
            e1 = e1q[q]
            if grp + 4 < 128:
                v_gs[grp + 4] = emit_gather(grp + 4)
                emit_A(grp + 4, v_gs[grp + 4])
            if grp + 2 < 128:
                emit_evac(grp + 2)
            for k in range(4):
                w = [(SS, s_evacs[grp]), psF2_free[q]] if k == 0 else []
                em("tensor", lambda e, k=k, d=e1, ps=psF2: e.matmul(
                    ps[:, k * 128:(k + 1) * 128], d[:, k * 128:(k + 1) * 128], h64i2,
                    start=True, stop=True), wait=w, inc=(TT_ if k == 3 else None))
            t_f2 = TT_.n
            e1_free[q] = (TT_, t_f2)
            if grp % 2 == 0:
                em("vector", lambda e, g8=grp * 8, ps=psF2: e.tensor_copy(
                    hid[:, :, g8:g8 + 8].rearrange("p c (q t) -> p q c t", q=4),
                    ps.rearrange("p (q c t) -> p q c t", q=4, c=64)),
                    wait=[(TT_, t_f2)], inc=VV)
                psF2_free[q] = (VV, VV.n)
            else:
                em("scalar", lambda e, g8=grp * 8, ps=psF2: e.activation(
                    hid[:, :, g8:g8 + 8].rearrange("p c (q t) -> p q c t", q=4),
                    ps.rearrange("p (q c t) -> p q c t", q=4, c=64), AF.Identity),
                    wait=[(TT_, t_f2)], inc=SS)
                psF2_free[q] = (SS, SS.n)
        v_fh = VV.n
        s_fh = SS.n

        if stop_after == "D":
            _emit_dumps(P, nc, dumps, dump_h, DUMPVIEWS, TT_, VV, SS, cOUT)
            _finalize(nc, block, P)
            return nc

        D_t, D_s, D_v = TT_.n, SS.n, VV.n
        w2_free = [(TT_, D_t), (TT_, D_t)]
        psW_free = [(SS, D_s), (SS, D_s)]
        ybfE_free = [(TT_, D_t), (TT_, D_t)]
        e_psT_free = [psF1_free[2], psF1_free[3]]
        s_yb = 0
        for o in range(16):
            opar = o % 2
            psW = (P0, P1)[opar]
            for qtr in range(4):
                par = qtr % 2
                em("gpsimd", lambda e, o=o, q=qtr, p=par: e.dma_start(
                    out=w2[p], in_=w2_h[o][:, q * 16:(q + 1) * 16, :]),
                    wait=[w2_free[par]], inc=cW2[par], inc_by=16)
                w0 = [(cW2[par], cW2[par].n), (VV, v_fh), (SS, s_fh)]
                if qtr == 0:
                    w0.append(psW_free[opar])
                for cc in range(16):
                    cg = qtr * 16 + cc
                    for nh in range(2):
                        em("tensor", lambda e, cc=cc, cg=cg, nh=nh, p=par, ps=psW: e.matmul(
                            ps[:, nh * 512:(nh + 1) * 512], w2[p][:, cc, :],
                            hid[:, cg, nh * 512:(nh + 1) * 512],
                            start=(cg == 0), stop=(cg == NC_ - 1)),
                            wait=(w0 if (cc == 0 and nh == 0) else []),
                            inc=(TT_ if (cc == 15 and nh == 1) else None))
                w2_free[par] = (TT_, TT_.n)
            t_w2 = TT_.n
            em("scalar", lambda e, p=opar, ps=psW: e.activation(ybfE[p], ps[:, 0:1024], AF.Identity),
               wait=[(TT_, t_w2), ybfE_free[opar]], inc=SS)
            s_ye = SS.n
            psW_free[opar] = (SS, s_ye)
            for grp in range(2):
                pp = psTpar; psTpar ^= 1
                ps_half = (P2[:, 0:512], P2[:, 512:1024])[pp]
                tf = (lambda k, g=grp, p=opar: ybfE[p][:, (g * 4 + k) * 128:(g * 4 + k + 1) * 128])
                wx = [(SS, s_ye)]
                if o < 2:
                    wx.append(d_psT_free[pp])
                tmm, s_yb = transpose4(ps_half, tf,
                                       ybuf[:, grp * 4:grp * 4 + 4, o * 128:(o + 1) * 128],
                                       wait_in=wx, evac_free_wait=[e_psT_free[pp]])
                e_psT_free[pp] = (SS, s_yb)
            ybfE_free[opar] = (TT_, TT_.n)
        t_eall = TT_.n
        xe_free = [(TT_, t_eall), (TT_, t_eall)]
        out_free = [(VV, 0), (VV, 0)]
        for t in range(NT):
            par = t % 2
            em("sync", lambda e, t=t, p=par: e.dma_start(out=x2stgE[p], in_=x2sp_h[t]),
               wait=[xe_free[par]], inc=cXE[par], inc_by=16)
            w_x = [(cXE[par], cXE[par].n)]
            s_nmr, r_ap, nmr_ap = ln_stats(ybuf[:, t, :], par, xn_tm[par],
                                           [(SS, s_yb)] if t == 0 else [])
            for k in range(4):
                scr = (scrB, scrC)[k % 2]
                em("scalar", lambda e, t=t, k=k, s=scr, r=r_ap, m=nmr_ap: e.activation(
                    s, ybuf[:, t, k * 512:(k + 1) * 512], AF.Identity, bias=m, scale=r),
                    wait=([(SS, s_nmr)] if k == 0 else [(VV, VV.n)]) + ([out_free[par]] if k == 0 else []),
                    inc=SS)
                em("vector", lambda e, k=k, p=par, s=scr: e.tensor_tensor(
                    outstg[p][:, k * 512:(k + 1) * 512], x2stgE[p][:, k * 512:(k + 1) * 512],
                    s, ALU.add), wait=[(SS, SS.n)] + (w_x if k == 0 else []), inc=VV)
            v_out = VV.n
            xe_free[par] = (VV, v_out)
            em("sync", lambda e, t=t, p=par: e.dma_start(out=out_h[t], in_=outstg[p]),
               wait=[(VV, v_out)], inc=cOUT, inc_by=16)
            out_free[par] = (cOUT, cOUT.n)
        em("sync", lambda e, n=cOUT.n: e.wait_ge(hOUT, n))

        _emit_dumps(P, nc, dumps, dump_h, DUMPVIEWS, TT_, VV, SS, cOUT)
        _finalize(nc, block, P)
        return nc


def _emit_dumps(P, nc, dumps, dump_h, views, TT_, VV, SS, cOUT):
    if not dumps:
        return
    for dn, shape, dt in dumps:
        v = views[dn]
        P.em("sync", lambda e, v=v, d=dump_h[dn]: e.dma_start(out=d, in_=v),
             wait=[(TT_, TT_.n), (VV, VV.n), (SS, SS.n)], inc=cOUT, inc_by=16)
    P.em("sync", lambda e, n=cOUT.n: e.wait_ge(cOUT.h, n))


def _finalize(nc, block, P):
    for eng in Prog.ENGINES:
        ops = P.q[eng]
        getattr(block, eng)(lambda e, ops=ops: [f(e) for f in ops])


def _rope_tables():
    inv = (1.0 / (10000.0 ** (np.arange(0, HD, 2, dtype=np.float32) / HD))).astype(np.float32)
    t = np.arange(S, dtype=np.float32)
    fr = np.outer(t, inv).astype(np.float32)
    emb = np.concatenate([fr, fr], axis=-1)
    return np.cos(emb).astype(np.float32), np.sin(emb).astype(np.float32)


def _popcount_sign(n):
    a = np.arange(n)
    m = a[:, None] & a[None, :]
    pc = np.zeros_like(m)
    while m.any():
        pc += m & 1
        m >>= 1
    return (1.0 - 2.0 * (pc & 1)).astype(np.float32)


def make_inputs(L0_attn_x, Wq, Wk, Wv, Wo, W1, W2, W3):
    x = np.asarray(L0_attn_x, dtype=np.float32)
    cosf, sinf = _rope_tables()

    wqkv_s = np.ascontiguousarray(
        np.concatenate([np.asarray(Wq), np.asarray(Wk), np.asarray(Wv)], axis=0)
        .T.reshape(16, 128, 3072)).astype(np.float32)
    wo_s = np.ascontiguousarray(np.asarray(Wo).T.reshape(16, 128, 2048)).astype(np.float32)
    w1t = np.asarray(W1).reshape(128, 64, 16, 128).transpose(1, 3, 2, 0)
    w3t = np.asarray(W3).reshape(128, 64, 16, 128).transpose(1, 3, 2, 0)
    w13_s = np.ascontiguousarray(np.stack([w1t, w3t], axis=2)).astype(np.float32)
    w2_s = np.ascontiguousarray(
        np.asarray(W2).reshape(16, 128, 128, 64).transpose(0, 2, 3, 1)).astype(np.float32)

    ident = np.eye(128, dtype=NPBF)
    h128 = _popcount_sign(128).astype(NPBF)
    h64i2 = np.kron(_popcount_sign(64), np.eye(2, dtype=np.float32)).astype(NPBF)
    tri = np.triu(np.ones((128, 128), dtype=np.float32))
    maskt = np.ascontiguousarray(np.repeat(tri[:, None, :], 4, axis=1)).astype(NPBF)

    shared = dict(wqkv_s=wqkv_s, wo_s=wo_s, w13_s=w13_s, w2_s=w2_s,
                  ident=ident, h128=h128, h64i2=h64i2, maskt=maskt)

    maps = []
    for core in range(8):
        b, j = core // 4, core % 4
        start = T * j
        xown = np.ascontiguousarray(x[b, start:start + T].reshape(NT, 128, DM))
        cosown = np.ascontiguousarray(cosf[start:start + T].reshape(NT, 128, HD)).astype(NPBF)
        sinown = np.ascontiguousarray(sinf[start:start + T].reshape(NT, 128, HD)).astype(NPBF)
        gm = np.zeros((128, 4), np.float32)
        gm[:, 0:j] = 1.0
        m = dict(shared)
        m.update(xown=xown, cosown=cosown, sinown=sinown, gmask=np.ascontiguousarray(gm))
        maps.append(m)
    return maps


def kernel(L0_attn_x, L0_attn_Wq, L0_attn_Wk, L0_attn_Wv, L0_attn_Wo,
           L0_ffn_W1, L0_ffn_W2, L0_ffn_W3):
    maps = make_inputs(L0_attn_x, L0_attn_Wq, L0_attn_Wk, L0_attn_Wv, L0_attn_Wo,
                       L0_ffn_W1, L0_ffn_W2, L0_ffn_W3)
    nc = build_nc()
    res = run_bass_kernel_spmd(nc, maps, core_ids=list(range(8)))
    outs = [r["out"].reshape(T, DM) for r in res.results]
    full = np.concatenate([np.stack(outs[0:4]).reshape(S, DM)[None],
                           np.stack(outs[4:8]).reshape(S, DM)[None]], axis=0)
    return full.astype(np.float32)


# version 17
# speedup vs baseline: 1.1503x; 1.1503x over previous
import numpy as np
import ml_dtypes
import concourse.bass as bass
import concourse.mybir as mybir
import concourse.bass_utils as _bass_utils
from concourse.bass_utils import run_bass_kernel_spmd

if not getattr(_bass_utils, "_ldwopt_patched", False):
    _orig_run_command = _bass_utils.run_command

    def _run_command_ldwopt(cmd, *a, **k):
        if isinstance(cmd, list):
            cmd = ["--enable-ldw-opt=true" if c == "--enable-ldw-opt=false" else c
                   for c in cmd]
        return _orig_run_command(cmd, *a, **k)

    _bass_utils.run_command = _run_command_ldwopt
    _bass_utils._ldwopt_patched = True

F32 = mybir.dt.float32
BF16 = mybir.dt.bfloat16
U8 = mybir.dt.uint8
AF = mybir.ActivationFunctionType
ALU = mybir.AluOpType
AX = mybir.AxisListType
NPBF = ml_dtypes.bfloat16

B, S, DM = 2, 4096, 2048
H, KVH, HD = 16, 4, 128
REP = H // KVH
T = 1024
NT = 8
NKV = 32
NPRE = NKV - NT
HID = 8192
NC_ = 64
EPS_LN = 1e-5
EPS_AT = 1e-6
FWHT_SCALE = float(1.0 / np.sqrt(np.float32(HID)))

KB = 1024


class Ctr:
    def __init__(self, handle, name):
        self.h = handle
        self.name = name
        self.n = 0


class Prog:
    ENGINES = ("sync", "gpsimd", "tensor", "vector", "scalar")

    def __init__(self):
        self.q = {e: [] for e in self.ENGINES}

    def em(self, eng, fn, wait=(), inc=None, inc_by=1):
        w = tuple((c.h, v) for (c, v) in wait)
        if inc is not None:
            inc.n += inc_by
        ih = None if inc is None else inc.h

        def run(e, fn=fn, w=w, ih=ih, ib=inc_by):
            for (s, v) in w:
                e.wait_ge(s, v)
            r = fn(e)
            if ih is not None:
                r.then_inc(ih, ib)
        self.q[eng].append(run)
        return None if inc is None else inc.n


AR = {}
def _slot(name, off, nbytes):
    AR[name] = (off, nbytes)
    return off + nbytes

_o = 0
_o = _slot("ident", _o, 256)
_o = _slot("h128", _o, 256)
_o = _slot("h64i2", _o, 256)
_o = _slot("maskt", _o, 1024)
_o = _slot("kvmask", _o, 128)
_o = _slot("cpool", _o, 32)
_o = _slot("stats", _o, 256)
_o = _slot("dent", _o, 128)
_o = ((_o + 127) // 128) * 128
_o = _slot("scrB", _o, 2048)
_o = _slot("scrC", _o, 2048)
_o = _slot("xn_tm", _o, 8192)
WORK = _o
_o = _slot("WORK", _o, 176 * KB)
_o = _slot("qftm", _o, 2048)
_o = _slot("fsc", _o, 2048)
_o = _slot("ckv", _o, 512)
_o = _slot("skv", _o, 512)
_o = _slot("sT", _o, 1024)
_o = _slot("gst3", _o, 6240)
_o = _slot("gbx", _o, 1040)
_o = _slot("gout", _o, 2080)
_o = _slot("astat", _o, 128)
ARENA_BYTES = _o
assert ARENA_BYTES <= 207 * KB + 896, ARENA_BYTES
AR["cosown"] = (WORK + 154 * KB, 2048)
AR["sinown"] = (WORK + 156 * KB, 2048)
AR["xnfm"] = (WORK + 158 * KB, 4096)

WA = {
    "wqkv": (0, 96 * KB),
    "qffm": (96 * KB, 32 * KB),
    "kffm": (128 * KB, 8 * KB),
    "vaug": (136 * KB, 8704),
    "saug": (136 * KB + 8704, 8704),
    "kftmc": (153 * KB, 1 * KB),
    "xstgA": (162 * KB, 8 * KB),
}
WB = {
    "cat": (0, 32 * KB),
    "stall": (32 * KB, 32 * KB),
}
WC = {
    "catfm": (32 * KB, 32 * KB),
    "wot": (64 * KB, 64 * KB),
    "xstg": (128 * KB, 16 * KB),
    "asb": (144 * KB, 16 * KB),
}
WD = {
    "xn3fm": (0, 32 * KB),
    "stg": (32 * KB, 16 * KB),
    "hid": (48 * KB, 128 * KB),
}
WE = {
    "w2": (0, 8 * KB),
    "ybuf": (8 * KB, 32 * KB),
    "ybf": (40 * KB, 4 * KB),
    "outstg": (48 * KB, 16 * KB),
    "x2stg": (64 * KB, 16 * KB),
}


def build_nc(stop_after=None, dumps=(), nkv_limit=None):
    nc = bass.Bass("TRN2", target_bir_lowering=False)

    xown_h = nc.dram_tensor("xown", [NT, 128, DM], F32, kind="ExternalInput").ap()
    cosown_h = nc.dram_tensor("cosown", [NT, 128, HD], BF16, kind="ExternalInput").ap()
    sinown_h = nc.dram_tensor("sinown", [NT, 128, HD], BF16, kind="ExternalInput").ap()
    gmask_h = nc.dram_tensor("gmask", [128, 4], F32, kind="ExternalInput").ap()
    ident_h = nc.dram_tensor("ident", [128, 128], BF16, kind="ExternalInput").ap()
    h128_h = nc.dram_tensor("h128", [128, 128], BF16, kind="ExternalInput").ap()
    h64i2_h = nc.dram_tensor("h64i2", [128, 128], BF16, kind="ExternalInput").ap()
    maskt_h = nc.dram_tensor("maskt", [128, 4, 128], BF16, kind="ExternalInput").ap()
    wqkv_h = nc.dram_tensor("wqkv_s", [16, 128, 3072], F32, kind="ExternalInput").ap()
    wo_h = nc.dram_tensor("wo_s", [16, 128, 2048], F32, kind="ExternalInput").ap()
    w13_h = nc.dram_tensor("w13_s", [NC_, 128, 2, 16, 128], F32, kind="ExternalInput").ap()
    w2_h = nc.dram_tensor("w2_s", [16, 128, NC_, 128], F32, kind="ExternalInput").ap()
    out_h = nc.dram_tensor("out", [NT, 128, DM], F32, kind="ExternalOutput").ap()
    x2sp_h = nc.dram_tensor("x2sp", [NT, 128, DM], F32, kind="Internal").ap()
    ccin_h = nc.dram_tensor("ccin", [128, 520], F32, kind="Internal").ap()
    ccout_h = nc.dram_tensor("ccout", [512, 520], F32, kind="Internal",
                             addr_space="Local").ap()

    dump_h = {}
    for dn, shape, dt in dumps:
        dump_h[dn] = nc.dram_tensor("dump_" + dn, shape, dt, kind="ExternalOutput").ap()

    P = Prog()

    from contextlib import ExitStack
    with ExitStack() as _stk:
        arena = _stk.enter_context(nc.sbuf_tensor("arena", [128, ARENA_BYTES], U8))
        P0 = _stk.enter_context(nc.psum_tensor("P0", [128, 1024], F32))
        P1 = _stk.enter_context(nc.psum_tensor("P1", [128, 1024], F32))
        P2 = _stk.enter_context(nc.psum_tensor("P2", [128, 1024], F32))
        P3 = _stk.enter_context(nc.psum_tensor("P3", [128, 1024], F32))
        def _sem(name):
            return _stk.enter_context(nc.semaphore(name))
        hT = _sem("sT_"); hV = _sem("sV_"); hS = _sem("sS_")
        hXK0 = _sem("cXK0"); hXK1 = _sem("cXK1")
        hCK0 = _sem("cCK0"); hCK1 = _sem("cCK1")
        hCON = _sem("cCON"); hWKV = _sem("cWKV"); hWQ = _sem("cWQ")
        hWO0 = _sem("cWO0"); hWO1 = _sem("cWO1")
        hW130 = _sem("cW130"); hW131 = _sem("cW131")
        hW20 = _sem("cW20"); hW21 = _sem("cW21")
        hXC0 = _sem("cXC0"); hXC1 = _sem("cXC1")
        hSP = _sem("cSP")
        hXD0 = _sem("cXD0"); hXD1 = _sem("cXD1")
        hXE0 = _sem("cXE0"); hXE1 = _sem("cXE1")
        hOUT = _sem("cOUT")
        block = _stk.enter_context(nc.Block())
        TT_ = Ctr(hT, "T"); VV = Ctr(hV, "V"); SS = Ctr(hS, "S")
        cXK = [Ctr(hXK0, "XK0"), Ctr(hXK1, "XK1")]
        cCK = [Ctr(hCK0, "CK0"), Ctr(hCK1, "CK1")]
        cCON = Ctr(hCON, "CON"); cWKV = Ctr(hWKV, "WKV"); cWQ = Ctr(hWQ, "WQ")
        cWO = [Ctr(hWO0, "WO0"), Ctr(hWO1, "WO1")]
        cW13 = [Ctr(hW130, "W130"), Ctr(hW131, "W131")]
        cW2 = [Ctr(hW20, "W20"), Ctr(hW21, "W21")]
        cXC = [Ctr(hXC0, "XC0"), Ctr(hXC1, "XC1")]
        cSP = Ctr(hSP, "SP")
        cXD = [Ctr(hXD0, "XD0"), Ctr(hXD1, "XD1")]
        cXE = [Ctr(hXE0, "XE0"), Ctr(hXE1, "XE1")]
        cOUT = Ctr(hOUT, "OUT")

        def vw(off, nbytes, dt, shape):
            ap = arena[:, off:off + nbytes].bitcast(dt)
            if len(shape) == 2:
                assert ap.shape[1] == shape[1], (ap.shape, shape)
                return ap
            pat = "p (" + " ".join(f"d{i}" for i in range(len(shape) - 1)) + ") -> p " + \
                  " ".join(f"d{i}" for i in range(len(shape) - 1))
            kw = {f"d{i}": shape[i + 1] for i in range(len(shape) - 1)}
            return ap.rearrange(pat, **kw)

        ident = vw(*AR["ident"], BF16, [128, 128])
        h128 = vw(*AR["h128"], BF16, [128, 128])
        h64i2 = vw(*AR["h64i2"], BF16, [128, 128])
        maskt = vw(*AR["maskt"], BF16, [128, 4, 128])
        gmask = vw(AR["kvmask"][0], 16, F32, [128, 4])
        cpool = vw(*AR["cpool"], F32, [128, 8])
        stats = vw(*AR["stats"], F32, [128, 64])
        dent = vw(*AR["dent"], F32, [128, 32])

        def SLT(k, c):
            col = (k * 2 + c) * 4
            return stats[:, col:col + 1]
        scrB = vw(*AR["scrB"], F32, [128, 512])
        scrC = vw(*AR["scrC"], F32, [128, 512])
        xn_tm = [vw(AR["xn_tm"][0] + p * 4096, 4096, BF16, [128, 2048]) for p in range(2)]
        xnfm = vw(AR["xnfm"][0], 4096, BF16, [128, 16, 128])
        cosown = vw(*AR["cosown"], BF16, [128, NT, HD])
        sinown = vw(*AR["sinown"], BF16, [128, NT, HD])
        qftm = [vw(AR["qftm"][0] + p * 1024, 1024, BF16, [128, 512]) for p in range(2)]
        fsc = [vw(AR["fsc"][0] + p * 1024, 1024, BF16, [128, 4, 128]) for p in range(2)]
        sTv = vw(*AR["sT"], BF16, [128, 4, 128])

        def wv(m, name, dt, shape, per=None, parity=2):
            off0, sz = m[name]
            if per is None:
                return vw(WORK + off0, sz, dt, shape)
            return [vw(WORK + off0 + p * per, per, dt, shape) for p in range(parity)]

        wqkv = wv(WA, "wqkv", BF16, [128, 16, 3072])
        qffm = wv(WA, "qffm", BF16, [128, 16, 1024])
        vaug = vw(WORK + WA["vaug"][0], 8320, BF16, [128, NT, 4, 130])
        saug = vw(WORK + WA["saug"][0], 8320, BF16, [128, NT, 4, 130])
        kffm = wv(WA, "kffm", BF16, [128, 4, 1024])
        kftmc = wv(WA, "kftmc", BF16, [128, 4, 128])
        xstgA = wv(WA, "xstgA", F32, [128, 2048])
        cat = wv(WB, "cat", BF16, [128, NT, 2048])
        gb = vw(*AR["gbx"], BF16, [128, 4, 130])
        gst = [vw(AR["gst3"][0] + p * 2080, 2080, F32, [128, 520]) for p in range(3)]
        gf32 = vw(*AR["gout"], F32, [128, 520])
        astat = vw(*AR["astat"], F32, [128, 32])
        kftmc2 = [kftmc, vw(AR["ckv"][0], 1024, BF16, [128, 4, 128])]
        stall = vw(WORK + WB["stall"][0], 32 * KB, BF16, [128, NT, 4, 4, 128])
        catfm = wv(WC, "catfm", BF16, [128, 16, 1024])
        wot = wv(WC, "wot", BF16, [128, 16, 2048])
        xstg = wv(WC, "xstg", F32, [128, 2048], per=8192)
        asb = wv(WC, "asb", F32, [128, 2048], per=8192)
        xn3fm = wv(WD, "xn3fm", BF16, [128, 16, 1024])
        w13 = wv(WD, "stg", BF16, [128, 2, 16, 128], per=8192)
        hid = wv(WD, "hid", BF16, [128, NC_, T])
        w2 = wv(WE, "w2", BF16, [128, 16, 128], per=4096)
        ybuf = wv(WE, "ybuf", BF16, [128, NT, 2048])
        ybfE = wv(WE, "ybf", BF16, [128, 1024], per=2048)
        outstg = wv(WE, "outstg", F32, [128, 2048], per=8192)
        x2stgE = wv(WE, "x2stg", F32, [128, 2048], per=8192)

        DUMPVIEWS = {
            "kffm": kffm, "vaug": vaug, "saug": saug, "qffm": qffm, "gb": gb,
            "cat": cat, "xn3fm": xn3fm, "hid": hid, "ybuf": ybuf, "catfm": catfm,
            "stats": stats, "sT": sTv, "stall": stall,
        }

        em = P.em

        em("sync", lambda e: e.dma_start(out=ident, in_=ident_h), inc=cCON, inc_by=16)
        em("sync", lambda e: e.dma_start(out=h128, in_=h128_h), inc=cCON, inc_by=16)
        em("sync", lambda e: e.dma_start(out=h64i2, in_=h64i2_h), inc=cCON, inc_by=16)
        em("sync", lambda e: e.dma_start(out=maskt, in_=maskt_h), inc=cCON, inc_by=16)
        em("sync", lambda e: e.dma_start(out=gmask, in_=gmask_h), inc=cCON, inc_by=16)
        for i in range(NT):
            em("sync", lambda e, i=i: e.dma_start(out=cosown[:, i, :], in_=cosown_h[i]), inc=cCON, inc_by=16)
            em("sync", lambda e, i=i: e.dma_start(out=sinown[:, i, :], in_=sinown_h[i]), inc=cCON, inc_by=16)
        n_con = cCON.n

        for col, val in ((0, EPS_LN), (1, EPS_AT), (2, FWHT_SCALE),
                         (3, 1.0 / DM), (4, -1.0 / DM), (5, -0.5)):
            em("vector", lambda e, c=col, v=val: e.memset(cpool[:, c:c + 1], v), inc=VV)
        em("vector", lambda e: e.memset(vaug[:, :, :, 128:129], 1.0), inc=VV)
        em("vector", lambda e: e.memset(vaug[:, :, :, 129:130], 0.0), inc=VV)
        em("vector", lambda e: e.memset(gb[:, :, 129:130], 0.0), inc=VV)
        em("vector", lambda e: e.memset(saug[:, 0, :, :], 0.0), inc=VV)
        v_init = VV.n

        for d in range(16):
            em("gpsimd", lambda e, d=d: e.dma_start(out=wqkv[:, d, :], in_=wqkv_h[d]), inc=cWKV, inc_by=16)
        n_wkv = cWKV.n

        def ln_stats(src, par, trash_bf, wait_src):
            em("vector", lambda e, s=src, p=par: e.reduce_sum(
                SLT(0, p), s, axis=AX.X), wait=wait_src, inc=VV)
            v_sum = VV.n
            em("scalar", lambda e, p=par: e.activation(
                SLT(1, p), SLT(0, p), AF.Identity, scale=cpool[:, 4:5]),
                wait=[(VV, v_sum), (VV, v_init)], inc=SS)
            s_negm = SS.n
            em("scalar", lambda e, s=src, t=trash_bf, p=par: e.activation(
                t, s, AF.Square, bias=SLT(1, p), accum_out=SLT(2, p)),
                wait=wait_src + [(SS, s_negm)], inc=SS)
            s_var = SS.n
            em("scalar", lambda e, p=par: e.activation(
                SLT(3, p), SLT(2, p), AF.Ln, scale=cpool[:, 3:4], bias=cpool[:, 0:1]),
                wait=[(SS, s_var)], inc=SS)
            em("scalar", lambda e, p=par: e.activation(
                SLT(4, p), SLT(3, p), AF.Exp, scale=cpool[:, 5:6]),
                wait=[(SS, SS.n)], inc=SS)
            s_r = SS.n
            em("scalar", lambda e, p=par: e.activation(
                SLT(5, p), SLT(1, p), AF.Identity, scale=SLT(4, p)),
                wait=[(SS, s_r)], inc=SS)
            s_nmr = SS.n
            return s_nmr, SLT(4, par), SLT(5, par)

        def transpose4(ps_half, in_tile_fn, dst_ap, wait_in, evac_free_wait, evac_wait=()):
            for k in range(4):
                w = list(wait_in) + (list(evac_free_wait) if k == 0 else [])
                em("tensor", lambda e, k=k, f=in_tile_fn, ps=ps_half: e.matmul(
                    ps[:, k * 128:(k + 1) * 128], f(k), ident, start=True, stop=True),
                    wait=w, inc=(TT_ if k == 3 else None))
            t_mm = TT_.n
            ps4 = ps_half.rearrange("p (a b) -> p a b", a=4)
            em("scalar", lambda e, ps=ps4, d=dst_ap: e.activation(d, ps, AF.Identity),
               wait=[(TT_, t_mm)] + list(evac_wait), inc=SS)
            return t_mm, SS.n

        state_ps = P3
        st_off = [(0, 0), (0, 192), (512, 0), (512, 192)]
        QBANKS = (P0[:, 0:512], P0[:, 512:1024], P1[:, 0:512], P1[:, 512:1024])

        cXA, cSO, cGI = cXK[0], cXK[1], cCK[0]
        cCC = cWQ
        psT_free = [(SS, 0), (SS, 0)]
        psQ_free = [(VV, 0)] * 4
        xn_free = [(TT_, 0), (TT_, 0)]
        xstg_free = (SS, 0)
        t_state = {}
        s_snap = {}
        psTpar = 0
        qftm_guard = [(TT_, 0), (TT_, 0)]
        scrB3 = scrB.rearrange("p (a b) -> p a b", a=4)
        scrC3 = scrC.rearrange("p (a b) -> p a b", a=4)

        t_kv_last = 0
        p1_free = []
        a1_pend = None

        def a1_flush(t, v_kf, s_ve):
            nonlocal psTpar
            pp = psTpar; psTpar ^= 1
            ps_half = (P2[:, 0:512], P2[:, 512:1024])[pp]
            _, s_kfe = transpose4(ps_half, (lambda k, t=t: kftmc2[t % 2][:, k, :]),
                                  kffm[:, :, t * 128:(t + 1) * 128],
                                  wait_in=[(VV, v_kf)], evac_free_wait=[psT_free[pp]])
            psT_free[pp] = (SS, s_kfe)
            if t == 0:
                s_snap[t] = 0
            else:
                wsn = [(TT_, t_state.get(t - 1, 0))]
                for halfsel in range(2):
                    srcp = state_ps[:, halfsel * 512:halfsel * 512 + 384].rearrange(
                        "p (g x) -> p g x", g=2)[:, :, 0:130]
                    dst = saug[:, t, halfsel * 2:halfsel * 2 + 2, :]
                    em("scalar", lambda e, s=srcp, d=dst: e.activation(d, s, AF.Identity),
                       wait=(wsn if halfsel == 0 else []), inc=(SS if halfsel == 1 else None))
                s_snap[t] = SS.n
            for g in range(4):
                half, off = st_off[g]
                stf = (t == 0 and off == 0)
                em("tensor", lambda e, g=g, h=half, o=off, stf=stf, t=t: e.matmul(
                    state_ps[:, h + o:h + o + 130], kftmc2[t % 2][:, g, :],
                    vaug[:, t, g, 0:130], start=stf, stop=(t == NT - 1)),
                    wait=([(VV, v_kf), (SS, s_ve), (SS, s_snap[t])] if g == 0 else []),
                    inc=(TT_ if g == 3 else None))
            t_state[t] = TT_.n

        spine = {}

        def a1_spine(t):
            nonlocal xstg_free
            par = t % 2
            em("sync", lambda e, t=t: e.dma_start(out=xstgA, in_=xown_h[t]),
               wait=[xstg_free], inc=cXA, inc_by=16)
            w_x = [(cXA, cXA.n)]
            s_nmr, r_ap, nmr_ap = ln_stats(xstgA, par, xn_tm[par], w_x + [xn_free[par]])
            em("scalar", lambda e, p=par, r=r_ap, m=nmr_ap: e.activation(
                xn_tm[p], xstgA, AF.Identity, bias=m, scale=r), wait=[(SS, s_nmr)], inc=SS)
            s_norm = SS.n
            xstg_free = (SS, s_norm)
            em("scalar", lambda e, t=t, m=nmr_ap: e.activation(
                astat[:, t:t + 1], m, AF.Identity), wait=[(SS, s_nmr)], inc=SS)
            em("scalar", lambda e, t=t, r=r_ap: e.activation(
                astat[:, 16 + t:17 + t], r, AF.Identity), inc=SS)
            spine[t] = s_norm

        a1_spine(0)
        for t in range(NT):
            par = t % 2
            s_norm = spine[t]
            s_fme = [0] * 4
            for grp in range(4):
                pp = psTpar; psTpar ^= 1
                ps_half = (P2[:, 0:512], P2[:, 512:1024])[pp]
                tf = (lambda k, g=grp, p=par: xn_tm[p][:, (g * 4 + k) * 128:(g * 4 + k + 1) * 128])
                _, s_fme[grp] = transpose4(
                    ps_half, tf, xnfm[:, grp * 4:grp * 4 + 4, :],
                    wait_in=[(SS, s_norm)] + ([(cCON, n_con)] if t == 0 else []),
                    evac_free_wait=[psT_free[pp]],
                    evac_wait=[(TT_, t_kv_last)])
                psT_free[pp] = (SS, s_fme[grp])
            xn_free[par] = (TT_, TT_.n)
            for d in range(16):
                w = [(SS, s_fme[d // 4])]
                if d == 0:
                    w += p1_free
                    if t == 0:
                        w.append((cWKV, n_wkv))
                em("tensor", lambda e, d=d: e.matmul(
                    P1[:, 0:512], xnfm[:, d, :], wqkv[:, d, 2048:2560],
                    start=(d == 0), stop=(d == 15)), wait=w)
                em("tensor", lambda e, d=d: e.matmul(
                    P1[:, 512:1024], xnfm[:, d, :], wqkv[:, d, 2560:3072],
                    start=(d == 0), stop=(d == 15)), inc=(TT_ if d == 15 else None))
            t_kv = TT_.n
            t_kv_last = t_kv
            if t + 1 < NT:
                a1_spine(t + 1)
            if a1_pend is not None:
                a1_flush(*a1_pend)
                a1_pend = None
            em("scalar", lambda e, t=t: e.activation(
                vaug[:, t, :, 0:128], P1[:, 512:1024].rearrange("p (a b) -> p a b", a=4),
                AF.Identity), wait=[(TT_, t_kv)] + ([(VV, v_init)] if t == 0 else []), inc=SS)
            s_ve = SS.n
            psK3 = P1[:, 0:512].rearrange("p (a b) -> p a b", a=4)
            cb = cosown[:, t, :].unsqueeze(1).broadcast_to([128, 4, HD])
            sb = sinown[:, t, :].unsqueeze(1).broadcast_to([128, 4, HD])
            em("vector", lambda e, s=psK3: e.tensor_scalar(
                scrB3[:, :, 0:64], s[:, :, 64:128], -1.0, None, ALU.mult),
                wait=[(TT_, t_kv)], inc=VV)
            em("vector", lambda e, s=psK3: e.tensor_copy(scrB3[:, :, 64:128], s[:, :, 0:64]), inc=VV)
            em("vector", lambda e, s=psK3, c=cb: e.tensor_tensor(scrC3, s, c, ALU.mult), inc=VV)
            v_kps = VV.n
            em("vector", lambda e, s=sb: e.tensor_tensor(scrB3, scrB3, s, ALU.mult), inc=VV)
            em("vector", lambda e: e.tensor_tensor(scrC, scrC, scrB, ALU.add), inc=VV)
            em("vector", lambda e: e.tensor_scalar(scrB, scrC, 0.0, None, ALU.min), inc=VV)
            v_min = VV.n
            em("scalar", lambda e: e.activation(scrB, scrB, AF.Exp), wait=[(VV, v_min)], inc=SS)
            s_exp = SS.n
            em("vector", lambda e, t=t: e.scalar_tensor_tensor(
                kftmc2[t % 2], scrC3, 0.0, scrB3, ALU.max, ALU.add),
                wait=[(SS, s_exp), (TT_, t_state.get(t - 2, 0))], inc=VV)
            v_kf = VV.n
            p1_free = [(VV, v_kps), (SS, s_ve)]
            a1_pend = (t, v_kf, s_ve)
        a1_flush(*a1_pend)
        a1_pend = None
        a1_s_ast = SS.n

        gf4 = gf32.rearrange("p (g x) -> p g x", g=4)
        for halfsel in range(2):
            srcp = state_ps[:, halfsel * 512:halfsel * 512 + 384].rearrange(
                "p (g x) -> p g x", g=2)[:, :, 0:130]
            em("scalar", lambda e, s=srcp, hs=halfsel: e.activation(
                gf4[:, hs * 2:hs * 2 + 2, :], s, AF.Identity),
                wait=([(TT_, t_state[NT - 1])] if halfsel == 0 else []),
                inc=(SS if halfsel == 1 else None))
        s_stot = SS.n
        em("sync", lambda e: e.dma_start(out=ccin_h, in_=gf32),
           wait=[(SS, s_stot)], inc=cSO, inc_by=16)
        em("gpsimd", lambda e: e.collective_compute(
            "AllGather", mybir.AluOpType.bypass, ins=[ccin_h], outs=[ccout_h],
            replica_groups=[[0, 1, 2, 3], [4, 5, 6, 7]]),
            wait=[(cSO, cSO.n)], inc=cCC)
        ccv = ccout_h.rearrange("(r p) f -> p r f", p=128)
        for k in range(3):
            em("gpsimd", lambda e, k=k: e.dma_start(out=gst[k], in_=ccv[:, k, :]),
               wait=([(cCC, 1)] if k == 0 else []), inc=cGI, inc_by=16)
        n_gi = cGI.n

        A1_t, A1_v = TT_.n, VV.n
        t_q_last = t_kv_last
        xn_free = [(TT_, A1_t), (TT_, A1_t)]
        qpairs = [(P0[:, 0:512], P0[:, 512:1024]),
                  (P1[:, 0:512], P1[:, 512:1024]),
                  (P3[:, 0:512], P3[:, 512:1024])]
        pair_free = [[(VV, A1_v)], p1_free, [(SS, s_stot)]]
        qftm4 = [qftm[0], qftm[1],
                 vw(AR["fsc"][0], 1024, BF16, [128, 512]),
                 vw(AR["fsc"][0] + 1024, 1024, BF16, [128, 512])]
        qftm_guard4 = [(TT_, 0)] * 4
        spine2 = {}

        def a2_load(t):
            nonlocal xstg_free
            em("sync", lambda e, t=t: e.dma_start(out=xstgA, in_=xown_h[t]),
               wait=[xstg_free], inc=cXA, inc_by=16)
            spine2[t] = [(cXA, cXA.n)]

        def a2_norm(t):
            nonlocal xstg_free
            par = t % 2
            em("scalar", lambda e, p=par, t=t: e.activation(
                xn_tm[p], xstgA, AF.Identity, bias=astat[:, t:t + 1],
                scale=astat[:, 16 + t:17 + t]),
                wait=spine2[t] + [xn_free[par], (SS, a1_s_ast)], inc=SS)
            spine2[t] = SS.n
            xstg_free = (SS, SS.n)

        a2_load(0)
        a2_norm(0)
        qf_pend = []
        rope_pend = None

        def qf_flush(t, vqs):
            nonlocal psTpar
            for g in range(4):
                pp = psTpar; psTpar ^= 1
                ps_half = (P2[:, 0:512], P2[:, 512:1024])[pp]
                tfq = (lambda k, g=g: qftm4[g][:, k * 128:(k + 1) * 128])
                tmm_qf, s_qt = transpose4(ps_half, tfq,
                                          qffm[:, g * 4:g * 4 + 4, t * 128:(t + 1) * 128],
                                          wait_in=[(VV, vqs[g])],
                                          evac_free_wait=[psT_free[pp]])
                qftm_guard4[g] = (TT_, tmm_qf)
                psT_free[pp] = (SS, s_qt)

        def emit_ropes(t, qb, t_qh):
            pa, pb = (2 * t) % 3, (2 * t + 1) % 3
            cb = cosown[:, t, :].unsqueeze(1).broadcast_to([128, 4, HD])
            sb = sinown[:, t, :].unsqueeze(1).broadcast_to([128, 4, HD])
            vqs = [0] * 4
            for g in range(4):
                ps3 = qb[g].rearrange("p (a b) -> p a b", a=4)
                em("vector", lambda e, s=ps3: e.tensor_scalar(
                    scrB3[:, :, 0:64], s[:, :, 64:128], -1.0, None, ALU.mult),
                    wait=[(TT_, t_qh[g // 2])], inc=VV)
                em("vector", lambda e, s=ps3: e.tensor_copy(scrB3[:, :, 64:128], s[:, :, 0:64]), inc=VV)
                em("vector", lambda e, s=ps3, c=cb: e.tensor_tensor(scrC3, s, c, ALU.mult), inc=VV)
                if g == 1:
                    pair_free[pa] = [(VV, VV.n)]
                elif g == 3:
                    pair_free[pb] = [(VV, VV.n)]
                em("vector", lambda e, s=sb: e.tensor_tensor(scrB3, scrB3, s, ALU.mult), inc=VV)
                em("vector", lambda e: e.tensor_tensor(scrC, scrC, scrB, ALU.add), inc=VV)
                em("vector", lambda e: e.tensor_scalar(scrB, scrC, 0.0, None, ALU.min), inc=VV)
                v_min = VV.n
                em("scalar", lambda e: e.activation(scrB, scrB, AF.Exp), wait=[(VV, v_min)], inc=SS)
                s_exp = SS.n
                em("vector", lambda e, g=g: e.scalar_tensor_tensor(
                    qftm4[g], scrC, 0.0, scrB, ALU.max, ALU.add),
                    wait=[(SS, s_exp), qftm_guard4[g]], inc=VV)
                vqs[g] = VV.n
            qf_pend.append((t, vqs))

        for t in range(NT):
            par = t % 2
            s_norm = spine2[t]
            if t + 1 < NT:
                a2_load(t + 1)
            s_fme = [0] * 4
            for grp in range(4):
                pp = psTpar; psTpar ^= 1
                ps_half = (P2[:, 0:512], P2[:, 512:1024])[pp]
                tf = (lambda k, g=grp, p=par: xn_tm[p][:, (g * 4 + k) * 128:(g * 4 + k + 1) * 128])
                _, s_fme[grp] = transpose4(
                    ps_half, tf, xnfm[:, grp * 4:grp * 4 + 4, :],
                    wait_in=[(SS, s_norm)],
                    evac_free_wait=[psT_free[pp]],
                    evac_wait=[(TT_, t_q_last)])
                psT_free[pp] = (SS, s_fme[grp])
            xn_free[par] = (TT_, TT_.n)
            if t + 1 < NT:
                a2_norm(t + 1)
            if qf_pend and qf_pend[0][0] <= t - 2:
                qf_flush(*qf_pend.pop(0))
            if rope_pend is not None:
                emit_ropes(*rope_pend)
                rope_pend = None
            pa, pb = (2 * t) % 3, (2 * t + 1) % 3
            qb = [qpairs[pa][0], qpairs[pa][1], qpairs[pb][0], qpairs[pb][1]]
            t_qh = [0, 0]
            for half in range(2):
                pidx = (pa, pb)[half]
                for d in range(16):
                    w = [(SS, s_fme[d // 4])]
                    if d == 0:
                        w = w + list(pair_free[pidx])
                    for gg in range(2):
                        g = half * 2 + gg
                        em("tensor", lambda e, d=d, g=g, qb=qb: e.matmul(
                            qb[g], xnfm[:, d, :], wqkv[:, d, g * 512:(g + 1) * 512],
                            start=(d == 0), stop=(d == 15)),
                            wait=(w if gg == 0 else []),
                            inc=(TT_ if (d == 15 and gg == 1) else None))
                t_qh[half] = TT_.n
            t_q_last = TT_.n
            rope_pend = (t, qb, t_qh)
        qf_flush(*qf_pend.pop(0))
        emit_ropes(*rope_pend)
        rope_pend = None
        psB_free = [pair_free[1][0], pair_free[1][0]]

        def emit_p1(t):
            nonlocal psB_free
            for g in range(4):
                pp = (t * 4 + g) % 2
                ps_half = (P1[:, 0:512], P1[:, 512:1024])[pp]
                em("tensor", lambda e, g=g, t=t, ps=ps_half: e.matmul(
                    ps, kffm[:, g, t * 128:(t + 1) * 128],
                    qffm[:, g * 4:g * 4 + 4, t * 128:(t + 1) * 128],
                    start=True, stop=True), wait=[psB_free[pp]], inc=TT_)
                t_sc = TT_.n
                em("vector", lambda e, g=g, t=t, ps=ps_half: e.tensor_tensor(
                    stall[:, t, g, :, :], ps.rearrange("p (a b) -> p a b", a=4),
                    maskt, ALU.mult),
                    wait=[(TT_, t_sc)] + ([(cCON, n_con)] if (t == 0 and g == 0) else []),
                    inc=VV)
                psB_free[pp] = (VV, VV.n)

        for t_p1 in range(7):
            emit_p1(t_p1)
        qf_flush(*qf_pend.pop(0))
        emit_p1(7)
        p3_free_A = pair_free[2][0]

        em("vector", lambda e: e.tensor_scalar(gst[0], gst[0], gmask[:, 0:1], None, ALU.mult),
           wait=[(cGI, n_gi), (cCON, n_con)], inc=VV)
        em("vector", lambda e: e.scalar_tensor_tensor(
            gst[0], gst[1], gmask[:, 1:2], gst[0], ALU.mult, ALU.add), wait=[(VV, VV.n)], inc=VV)
        em("vector", lambda e: e.scalar_tensor_tensor(
            gst[0], gst[2], gmask[:, 2:3], gst[0], ALU.mult, ALU.add), wait=[(VV, VV.n)], inc=VV)
        em("vector", lambda e: e.tensor_copy(
            gb[:, :, 0:130], gst[0].rearrange("p (g x) -> p g x", g=4)),
            wait=[(VV, VV.n)], inc=VV)
        v_gb = VV.n

        if stop_after == "A":
            _emit_dumps(P, nc, dumps, dump_h, DUMPVIEWS, TT_, VV, SS, cOUT)
            _finalize(nc, block, P)
            return nc

        A_t, A_v, A_s = TT_.n, VV.n, SS.n

        psU_free = [(VV, A_v), (VV, A_v), (VV, A_v), (VV, A_v)]
        for t in range(NT):
            for g in range(4):
                u = (t * 4 + g) % 2
                PU = (P3, P0)[u]
                dof = u * 2
                for h in range(4):
                    half, off = st_off[h]
                    w0 = ([psU_free[2 * u], (VV, v_gb)] if h == 0 else
                          ([psU_free[2 * u + 1]] if h == 2 else []))
                    em("tensor", lambda e, h=h, hf=half, o=off, g=g, t=t, PU=PU: e.matmul(
                        PU[:, hf + o:hf + o + 130], stall[:, t, g, h, :], vaug[:, t, g, 0:130],
                        start=(o == 0), stop=False), wait=w0)
                    em("tensor", lambda e, h=h, hf=half, o=off, g=g, t=t, PU=PU: e.matmul(
                        PU[:, hf + o:hf + o + 130], qffm[:, g * 4 + h, t * 128:(t + 1) * 128],
                        saug[:, t, g, 0:130], start=False, stop=False))
                    em("tensor", lambda e, h=h, hf=half, o=off, g=g, t=t, PU=PU: e.matmul(
                        PU[:, hf + o:hf + o + 130], qffm[:, g * 4 + h, t * 128:(t + 1) * 128],
                        gb[:, g, 0:130], start=False, stop=True), inc=(TT_ if h == 3 else None))
                t_nm = TT_.n
                for hh in range(2):
                    srcd = PU[:, hh * 512:hh * 512 + 384].rearrange(
                        "p (g x) -> p g x", g=2)[:, :, 128:129].rearrange("p a b -> p (a b)")
                    em("scalar", lambda e, hh=hh, s=srcd, do=dof: e.activation(
                        dent[:, (do + hh) * 4:(do + hh) * 4 + 2], s, AF.Identity,
                        bias=cpool[:, 1:2]),
                        wait=([(TT_, t_nm)] if hh == 0 else []), inc=SS)
                s_den = SS.n
                em("vector", lambda e, do=dof: e.reciprocal(
                    dent[:, 16 + do * 4:18 + do * 4], dent[:, do * 4:do * 4 + 2]),
                   wait=[(SS, s_den)], inc=VV)
                em("vector", lambda e, do=dof: e.reciprocal(
                    dent[:, 20 + do * 4:22 + do * 4], dent[:, do * 4 + 4:do * 4 + 6]), inc=VV)
                v_rec = VV.n
                for h in range(4):
                    half, off = st_off[h]
                    rcol = (16 + h if h < 2 else 18 + h) + dof * 4
                    em("vector", lambda e, h=h, hf=half, o=off, g=g, t=t, rc=rcol, PU=PU: e.tensor_scalar(
                        cat[:, t, (g * 4 + h) * 128:(g * 4 + h + 1) * 128],
                        PU[:, hf + o:hf + o + 128], dent[:, rc:rc + 1], None, ALU.mult),
                        wait=([(VV, v_rec)] if h == 0 else []), inc=(VV if h == 3 else None))
                psU_free[2 * u] = (VV, VV.n)
                psU_free[2 * u + 1] = (VV, VV.n)

        v_scan_end = VV.n
        t_p2 = TT_.n
        for t in range(NT):
            for grp in range(4):
                pp = psTpar; psTpar ^= 1
                ps_half = (P2[:, 0:512], P2[:, 512:1024])[pp]
                for k in range(4):
                    w = ([(VV, v_scan_end)] if (t == 0 and grp == 0) else []) +                         ([psT_free[pp]] if k == 0 else [])
                    em("tensor", lambda e, k=k, t=t, gg=grp, ps=ps_half: e.matmul(
                        ps[:, k * 128:(k + 1) * 128],
                        cat[:, t, (gg * 4 + k) * 128:(gg * 4 + k + 1) * 128],
                        ident, start=True, stop=True),
                        wait=w, inc=(TT_ if k == 3 else None))
                t_mm = TT_.n
                ps4 = ps_half.rearrange("p (a b) -> p a b", a=4)
                dst = catfm[:, grp * 4:grp * 4 + 4, t * 128:(t + 1) * 128]
                if grp % 2 == 0:
                    em("scalar", lambda e, ps=ps4, d=dst: e.activation(d, ps, AF.Identity),
                       wait=[(TT_, t_mm)], inc=SS)
                    psT_free[pp] = (SS, SS.n)
                else:
                    em("vector", lambda e, ps=ps4, d=dst: e.tensor_copy(d, ps),
                       wait=[(TT_, t_mm)], inc=VV)
                    psT_free[pp] = (VV, VV.n)

        if stop_after == "B":
            _emit_dumps(P, nc, dumps, dump_h, DUMPVIEWS, TT_, VV, SS, cOUT)
            _finalize(nc, block, P)
            return nc

        B_t, B_v, B_s = TT_.n, VV.n, SS.n
        wot_cnt = []
        for f in range(16):
            w = [(TT_, A_t)] if f == 0 else ([(TT_, t_p2)] if f == 8 else [])
            em("gpsimd", lambda e, f=f: e.dma_start(out=wot[:, f, :], in_=wo_h[f]),
               wait=w, inc=cWO[0], inc_by=16)
            wot_cnt.append(cWO[0].n)
        n_wot = cWO[0].n

        xc_free = [[(TT_, B_t)], [(TT_, B_t)]]
        asb_free = [(SS, 0), (SS, 0)]
        psWo_free = [psU_free[3], (VV, A_v)]
        c_psT_free = [psT_free[0], psT_free[1]]
        s_fm3 = 0
        xp_pend = []
        cv_prev = [0] * 4

        def emit_xp(t, par, s_norm3):
            nonlocal psTpar, s_fm3
            for grp in range(4):
                pp = psTpar; psTpar ^= 1
                ps_half = (P2[:, 0:512], P2[:, 512:1024])[pp]
                tf = (lambda k, g=grp, p=par: xn_tm[p][:, (g * 4 + k) * 128:(g * 4 + k + 1) * 128])
                tmm, sf = transpose4(ps_half, tf,
                                     xn3fm[:, grp * 4:grp * 4 + 4, t * 128:(t + 1) * 128],
                                     wait_in=[(SS, s_norm3)], evac_free_wait=[c_psT_free[pp]])
                c_psT_free[pp] = (SS, sf)
                s_fm3 = sf
            xn_free[par] = (TT_, TT_.n)

        for t in range(NT):
            par = t % 2
            em("sync", lambda e, t=t, p=par: e.dma_start(out=xstg[p], in_=xown_h[t]),
               wait=list(xc_free[par]), inc=cXC[par], inc_by=16)
            w_x = [(cXC[par], cXC[par].n)]
            s_evac = 0
            for coh in range(2):
                psW = (P0, P1)[coh]
                for f in range(16):
                    w0 = []
                    if f == 0:
                        w0 = [psWo_free[coh], (cWO[0], wot_cnt[0])]
                        if t == 0 and coh == 0:
                            w0.append((SS, B_s))
                    elif t == 0:
                        w0 = [(cWO[0], wot_cnt[f])]
                    for nh in range(2):
                        em("tensor", lambda e, f=f, nh=nh, t=t, coh=coh, ps=psW: e.matmul(
                            ps[:, nh * 512:(nh + 1) * 512],
                            catfm[:, f, t * 128:(t + 1) * 128],
                            wot[:, f, coh * 1024 + nh * 512:coh * 1024 + (nh + 1) * 512],
                            start=(f == 0), stop=(f == 15)),
                            wait=(w0 if nh == 0 else []),
                            inc=(TT_ if (f == 15 and nh == 1) else None))
                t_wo = TT_.n
                wev = [(TT_, t_wo)]
                if coh == 0:
                    wev.append(asb_free[par])
                em("scalar", lambda e, coh=coh, p=par, ps=psW: e.activation(
                    asb[p][:, coh * 1024:(coh + 1) * 1024], ps[:, 0:1024], AF.Identity,
                    accum_out=SLT(6, coh)), wait=wev, inc=SS)
                psWo_free[coh] = (SS, SS.n)
            s_evac = SS.n
            if xp_pend and xp_pend[0][0] < t:
                emit_xp(*xp_pend.pop(0))
            em("scalar", lambda e, p=par: e.activation(
                SLT(0, p), SLT(6, 0), AF.Identity, bias=SLT(6, 1)),
                wait=[(SS, s_evac)], inc=SS)
            s_sum = SS.n
            em("scalar", lambda e, p=par: e.activation(
                SLT(1, p), SLT(0, p), AF.Identity, scale=cpool[:, 4:5]),
                wait=[(SS, s_sum)], inc=SS)
            s_negm = SS.n
            em("scalar", lambda e, p=par: e.activation(
                xn_tm[p], asb[p], AF.Square, bias=SLT(1, p), accum_out=SLT(2, p)),
                wait=[(SS, s_negm), xn_free[par]], inc=SS)
            s_var = SS.n
            em("scalar", lambda e, p=par: e.activation(
                SLT(3, p), SLT(2, p), AF.Ln, scale=cpool[:, 3:4], bias=cpool[:, 0:1]),
                wait=[(SS, s_var)], inc=SS)
            em("scalar", lambda e, p=par: e.activation(
                SLT(4, p), SLT(3, p), AF.Exp, scale=cpool[:, 5:6]),
                wait=[(SS, SS.n)], inc=SS)
            s_r = SS.n
            em("scalar", lambda e, p=par: e.activation(
                SLT(5, p), SLT(1, p), AF.Identity, scale=SLT(4, p)),
                wait=[(SS, s_r)], inc=SS)
            s_nmr = SS.n
            v_adds = [0] * 4
            for k in range(4):
                scr = (scrB, scrC)[k % 2]
                wn = [(SS, s_nmr)] if k == 0 else []
                wn.append((VV, v_adds[k - 2] if k >= 2 else cv_prev[k + 2]))
                em("scalar", lambda e, t=t, k=k, s=scr, p=par: e.activation(
                    s, asb[p][:, k * 512:(k + 1) * 512], AF.Identity,
                    bias=SLT(5, p), scale=SLT(4, p)), wait=wn, inc=SS)
                em("vector", lambda e, k=k, p=par, s=scr: e.tensor_tensor(
                    xstg[p][:, k * 512:(k + 1) * 512], xstg[p][:, k * 512:(k + 1) * 512],
                    s, ALU.add), wait=[(SS, SS.n)] + (w_x if k == 0 else []), inc=VV)
                v_adds[k] = VV.n
            cv_prev = v_adds
            v_x2 = VV.n
            asb_free[par] = (SS, SS.n)
            em("sync", lambda e, t=t, p=par: e.dma_start(out=x2sp_h[t], in_=xstg[p]),
               wait=[(VV, v_x2)], inc=cSP, inc_by=16)
            s_nmr3, r3, nmr3 = ln_stats(xstg[par], par, xn_tm[par], [(VV, v_x2)])
            em("scalar", lambda e, p=par, r=r3, m=nmr3: e.activation(
                xn_tm[p], xstg[p], AF.Identity, bias=m, scale=r), wait=[(SS, s_nmr3)], inc=SS)
            s_norm3 = SS.n
            xc_free[par] = [(cSP, cSP.n), (SS, s_norm3)]
            xp_pend.append((t, par, s_norm3))
        for t, par, s_norm3 in xp_pend:
            emit_xp(t, par, s_norm3)
        n_sp = cSP.n
        d_psT_free = c_psT_free

        if stop_after == "C":
            _emit_dumps(P, nc, dumps, dump_h, DUMPVIEWS, TT_, VV, SS, cOUT)
            _finalize(nc, block, P)
            return nc

        s_d0, v_d0 = SS.n, VV.n
        w13_free = [(TT_, t_wo), (TT_, t_wo)]
        psAB_free = [(SS, s_d0), (SS, s_d0)]
        v_hid = 0
        for c in range(NC_):
            par = c % 2
            wfill = [w13_free[par]]
            em("gpsimd", lambda e, c=c, p=par: e.dma_start(out=w13[p], in_=w13_h[c]),
               wait=wfill, inc=cW13[par], inc_by=16)
            psA = (P0, P2)[par]
            psB = (P1, P3)[par]
            w0 = [(cW13[par], cW13[par].n), (SS, s_fm3), psAB_free[par]]
            for w13i in range(2):
                ps = (psA, psB)[w13i]
                for d in range(16):
                    for nh in range(2):
                        em("tensor", lambda e, w=w13i, d=d, nh=nh, p=par, ps=ps: e.matmul(
                            ps[:, nh * 512:(nh + 1) * 512], w13[p][:, w, d, :],
                            xn3fm[:, d, nh * 512:(nh + 1) * 512], start=(d == 0), stop=(d == 15)),
                            wait=(w0 if (w13i == 0 and d == 0 and nh == 0) else []),
                            inc=(TT_ if (w13i == 1 and d == 15 and nh == 1) else None))
            t_13 = TT_.n
            w13_free[par] = (TT_, t_13)
            for nh in range(2):
                scr = (scrB, scrC)[nh]
                em("scalar", lambda e, s=scr, nh=nh, ps=psA: e.activation(
                    s, ps[:, nh * 512:(nh + 1) * 512], AF.Silu),
                    wait=[(TT_, t_13)] if nh == 0 else [], inc=(SS if nh == 1 else None))
            s_si = SS.n
            for nh in range(2):
                scr = (scrB, scrC)[nh]
                em("vector", lambda e, c=c, nh=nh, s=scr, ps=psB: e.tensor_tensor(
                    hid[:, c, nh * 512:(nh + 1) * 512], s, ps[:, nh * 512:(nh + 1) * 512],
                    ALU.mult), wait=[(SS, s_si)] if nh == 0 else [], inc=(VV if nh == 1 else None))
            v_hid = VV.n
            psAB_free[par] = (VV, v_hid)
        D2_s0 = SS.n
        PS1Q = [P0[:, 0:512], P0[:, 512:1024], P2[:, 0:512], P2[:, 512:1024]]
        PS2Q = [P1[:, 0:512], P1[:, 512:1024], P3[:, 0:512], P3[:, 512:1024]]
        e1q = [qftm[0], qftm[1],
               vw(AR["scrB"][0], 1024, BF16, [128, 512]),
               vw(AR["scrB"][0] + 1024, 1024, BF16, [128, 512])]
        fscq = [fsc[0], fsc[1],
                vw(AR["scrC"][0], 1024, BF16, [128, 4, 128]),
                vw(AR["scrC"][0] + 1024, 1024, BF16, [128, 4, 128])]
        psF1_free = [(SS, D2_s0), (SS, D2_s0), (SS, D2_s0), (SS, D2_s0)]
        psF2_free = [(VV, v_hid), (VV, v_hid), (VV, v_hid), (VV, v_hid)]
        e1_free = [(TT_, A_t), (TT_, A_t), (VV, v_hid), (VV, v_hid)]
        fsc_free = [(TT_, A_t), (TT_, A_t), (VV, v_hid), (VV, v_hid)]

        def emit_gather(grp):
            q = grp % 4
            em("vector", lambda e, g8=grp * 8, q=q: e.tensor_copy(
                fscq[q].rearrange("p q (c t) -> p q c t", c=64),
                hid[:, :, g8:g8 + 8].rearrange("p c (q t) -> p q c t", q=4)),
                wait=[fsc_free[q]], inc=VV)
            return VV.n

        t_f1s = {}

        def emit_A(grp, v_g):
            q = grp % 4
            psF1 = PS1Q[q]
            for k in range(4):
                w = [(VV, v_g), psF1_free[q]] if k == 0 else []
                em("tensor", lambda e, k=k, q=q, ps=psF1: e.matmul(
                    ps[:, k * 128:(k + 1) * 128], fscq[q][:, k, :], h128,
                    start=True, stop=True), wait=w, inc=(TT_ if k == 3 else None))
            t_f1s[grp] = TT_.n
            fsc_free[q] = (TT_, TT_.n)

        s_evacs = {}

        def emit_evac(grp):
            q = grp % 4
            em("scalar", lambda e, d=e1q[q], ps=PS1Q[q]: e.activation(
                d, ps, AF.Identity, scale=cpool[:, 2:3]),
                wait=[(TT_, t_f1s[grp]), e1_free[q]] + ([(VV, v_init)] if grp == 0 else []), inc=SS)
            s_evacs[grp] = SS.n
            psF1_free[q] = (SS, SS.n)

        v_gs = {}
        for g0 in range(4):
            v_gs[g0] = emit_gather(g0)
            emit_A(g0, v_gs[g0])
        emit_evac(0)
        emit_evac(1)
        for grp in range(128):
            q = grp % 4
            psF2 = PS2Q[q]
            e1 = e1q[q]
            if grp + 4 < 128:
                v_gs[grp + 4] = emit_gather(grp + 4)
                emit_A(grp + 4, v_gs[grp + 4])
            if grp + 2 < 128:
                emit_evac(grp + 2)
            for k in range(4):
                w = [(SS, s_evacs[grp]), psF2_free[q]] if k == 0 else []
                em("tensor", lambda e, k=k, d=e1, ps=psF2: e.matmul(
                    ps[:, k * 128:(k + 1) * 128], d[:, k * 128:(k + 1) * 128], h64i2,
                    start=True, stop=True), wait=w, inc=(TT_ if k == 3 else None))
            t_f2 = TT_.n
            e1_free[q] = (TT_, t_f2)
            if grp % 4 == 0:
                em("vector", lambda e, g8=grp * 8, ps=psF2: e.tensor_copy(
                    hid[:, :, g8:g8 + 8].rearrange("p c (q t) -> p q c t", q=4),
                    ps.rearrange("p (q c t) -> p q c t", q=4, c=64)),
                    wait=[(TT_, t_f2)], inc=VV)
                psF2_free[q] = (VV, VV.n)
            else:
                em("scalar", lambda e, g8=grp * 8, ps=psF2: e.activation(
                    hid[:, :, g8:g8 + 8].rearrange("p c (q t) -> p q c t", q=4),
                    ps.rearrange("p (q c t) -> p q c t", q=4, c=64), AF.Identity),
                    wait=[(TT_, t_f2)], inc=SS)
                psF2_free[q] = (SS, SS.n)
        v_fh = VV.n
        s_fh = SS.n

        if stop_after == "D":
            _emit_dumps(P, nc, dumps, dump_h, DUMPVIEWS, TT_, VV, SS, cOUT)
            _finalize(nc, block, P)
            return nc

        D_t, D_s, D_v = TT_.n, SS.n, VV.n
        w2_free = [(TT_, D_t), (TT_, D_t)]
        psW_free = [(SS, D_s), (SS, D_s)]
        ybfE_free = [(TT_, D_t), (TT_, D_t)]
        e_psT_free = [psF1_free[2], psF1_free[3]]
        s_yb = 0
        for o in range(16):
            opar = o % 2
            psW = (P0, P1)[opar]
            for qtr in range(4):
                par = qtr % 2
                em("gpsimd", lambda e, o=o, q=qtr, p=par: e.dma_start(
                    out=w2[p], in_=w2_h[o][:, q * 16:(q + 1) * 16, :]),
                    wait=[w2_free[par]], inc=cW2[par], inc_by=16)
                w0 = [(cW2[par], cW2[par].n), (VV, v_fh), (SS, s_fh)]
                if qtr == 0:
                    w0.append(psW_free[opar])
                for cc in range(16):
                    cg = qtr * 16 + cc
                    for nh in range(2):
                        em("tensor", lambda e, cc=cc, cg=cg, nh=nh, p=par, ps=psW: e.matmul(
                            ps[:, nh * 512:(nh + 1) * 512], w2[p][:, cc, :],
                            hid[:, cg, nh * 512:(nh + 1) * 512],
                            start=(cg == 0), stop=(cg == NC_ - 1)),
                            wait=(w0 if (cc == 0 and nh == 0) else []),
                            inc=(TT_ if (cc == 15 and nh == 1) else None))
                w2_free[par] = (TT_, TT_.n)
            t_w2 = TT_.n
            em("scalar", lambda e, p=opar, ps=psW: e.activation(ybfE[p], ps[:, 0:1024], AF.Identity),
               wait=[(TT_, t_w2), ybfE_free[opar]], inc=SS)
            s_ye = SS.n
            psW_free[opar] = (SS, s_ye)
            for grp in range(2):
                pp = psTpar; psTpar ^= 1
                ps_half = (P2[:, 0:512], P2[:, 512:1024])[pp]
                tf = (lambda k, g=grp, p=opar: ybfE[p][:, (g * 4 + k) * 128:(g * 4 + k + 1) * 128])
                wx = [(SS, s_ye)]
                if o < 2:
                    wx.append(d_psT_free[pp])
                tmm, s_yb = transpose4(ps_half, tf,
                                       ybuf[:, grp * 4:grp * 4 + 4, o * 128:(o + 1) * 128],
                                       wait_in=wx, evac_free_wait=[e_psT_free[pp]])
                e_psT_free[pp] = (SS, s_yb)
            ybfE_free[opar] = (TT_, TT_.n)
        t_eall = TT_.n
        xe_free = [(TT_, t_eall), (TT_, t_eall)]
        out_free = [(VV, 0), (VV, 0)]
        ev_prev = [0] * 4
        v_red = {}

        def e2_reduce(t, w0=()):
            em("vector", lambda e, t=t, p=t % 2: e.reduce_sum(
                SLT(0, p), ybuf[:, t, :], axis=AX.X), wait=list(w0), inc=VV)
            v_red[t] = VV.n

        e2_reduce(0, [(SS, s_yb)])
        for t in range(NT):
            par = t % 2
            em("sync", lambda e, t=t, p=par: e.dma_start(out=x2stgE[p], in_=x2sp_h[t]),
               wait=[xe_free[par]], inc=cXE[par], inc_by=16)
            w_x = [(cXE[par], cXE[par].n)]
            em("scalar", lambda e, p=par: e.activation(
                SLT(1, p), SLT(0, p), AF.Identity, scale=cpool[:, 4:5]),
                wait=[(VV, v_red[t]), (VV, v_init)], inc=SS)
            s_negm = SS.n
            em("scalar", lambda e, t=t, p=par: e.activation(
                xn_tm[p], ybuf[:, t, :], AF.Square, bias=SLT(1, p), accum_out=SLT(2, p)),
                wait=[(SS, s_negm)], inc=SS)
            s_var = SS.n
            em("scalar", lambda e, p=par: e.activation(
                SLT(3, p), SLT(2, p), AF.Ln, scale=cpool[:, 3:4], bias=cpool[:, 0:1]),
                wait=[(SS, s_var)], inc=SS)
            em("scalar", lambda e, p=par: e.activation(
                SLT(4, p), SLT(3, p), AF.Exp, scale=cpool[:, 5:6]),
                wait=[(SS, SS.n)], inc=SS)
            s_r = SS.n
            em("scalar", lambda e, p=par: e.activation(
                SLT(5, p), SLT(1, p), AF.Identity, scale=SLT(4, p)),
                wait=[(SS, s_r)], inc=SS)
            s_nmr = SS.n
            r_ap, nmr_ap = SLT(4, par), SLT(5, par)
            if t + 1 < NT:
                e2_reduce(t + 1)
            v_adds = [0] * 4
            for k in range(4):
                scr = (scrB, scrC)[k % 2]
                wn = [(SS, s_nmr)] if k == 0 else []
                wn.append((VV, v_adds[k - 2] if k >= 2 else ev_prev[k + 2]))
                if k == 0:
                    wn.append(out_free[par])
                em("scalar", lambda e, t=t, k=k, s=scr, r=r_ap, m=nmr_ap: e.activation(
                    s, ybuf[:, t, k * 512:(k + 1) * 512], AF.Identity, bias=m, scale=r),
                    wait=wn, inc=SS)
                em("vector", lambda e, k=k, p=par, s=scr: e.tensor_tensor(
                    outstg[p][:, k * 512:(k + 1) * 512], x2stgE[p][:, k * 512:(k + 1) * 512],
                    s, ALU.add), wait=[(SS, SS.n)] + (w_x if k == 0 else []), inc=VV)
                v_adds[k] = VV.n
            ev_prev = v_adds
            v_out = VV.n
            xe_free[par] = (VV, v_out)
            em("sync", lambda e, t=t, p=par: e.dma_start(out=out_h[t], in_=outstg[p]),
               wait=[(VV, v_out)], inc=cOUT, inc_by=16)
            out_free[par] = (cOUT, cOUT.n)
        em("sync", lambda e, n=cOUT.n: e.wait_ge(hOUT, n))

        _emit_dumps(P, nc, dumps, dump_h, DUMPVIEWS, TT_, VV, SS, cOUT)
        _finalize(nc, block, P)
        return nc


def _emit_dumps(P, nc, dumps, dump_h, views, TT_, VV, SS, cOUT):
    if not dumps:
        return
    for dn, shape, dt in dumps:
        v = views[dn]
        P.em("sync", lambda e, v=v, d=dump_h[dn]: e.dma_start(out=d, in_=v),
             wait=[(TT_, TT_.n), (VV, VV.n), (SS, SS.n)], inc=cOUT, inc_by=16)
    P.em("sync", lambda e, n=cOUT.n: e.wait_ge(cOUT.h, n))


def _finalize(nc, block, P):
    for eng in Prog.ENGINES:
        ops = P.q[eng]
        getattr(block, eng)(lambda e, ops=ops: [f(e) for f in ops])


def _rope_tables():
    inv = (1.0 / (10000.0 ** (np.arange(0, HD, 2, dtype=np.float32) / HD))).astype(np.float32)
    t = np.arange(S, dtype=np.float32)
    fr = np.outer(t, inv).astype(np.float32)
    emb = np.concatenate([fr, fr], axis=-1)
    return np.cos(emb).astype(np.float32), np.sin(emb).astype(np.float32)


def _popcount_sign(n):
    a = np.arange(n)
    m = a[:, None] & a[None, :]
    pc = np.zeros_like(m)
    while m.any():
        pc += m & 1
        m >>= 1
    return (1.0 - 2.0 * (pc & 1)).astype(np.float32)


def make_inputs(L0_attn_x, Wq, Wk, Wv, Wo, W1, W2, W3):
    x = np.asarray(L0_attn_x, dtype=np.float32)
    cosf, sinf = _rope_tables()

    wqkv_s = np.ascontiguousarray(
        np.concatenate([np.asarray(Wq), np.asarray(Wk), np.asarray(Wv)], axis=0)
        .T.reshape(16, 128, 3072)).astype(np.float32)
    wo_s = np.ascontiguousarray(np.asarray(Wo).T.reshape(16, 128, 2048)).astype(np.float32)
    w1t = np.asarray(W1).reshape(128, 64, 16, 128).transpose(1, 3, 2, 0)
    w3t = np.asarray(W3).reshape(128, 64, 16, 128).transpose(1, 3, 2, 0)
    w13_s = np.ascontiguousarray(np.stack([w1t, w3t], axis=2)).astype(np.float32)
    w2_s = np.ascontiguousarray(
        np.asarray(W2).reshape(16, 128, 128, 64).transpose(0, 2, 3, 1)).astype(np.float32)

    ident = np.eye(128, dtype=NPBF)
    h128 = _popcount_sign(128).astype(NPBF)
    h64i2 = np.kron(_popcount_sign(64), np.eye(2, dtype=np.float32)).astype(NPBF)
    tri = np.triu(np.ones((128, 128), dtype=np.float32))
    maskt = np.ascontiguousarray(np.repeat(tri[:, None, :], 4, axis=1)).astype(NPBF)

    shared = dict(wqkv_s=wqkv_s, wo_s=wo_s, w13_s=w13_s, w2_s=w2_s,
                  ident=ident, h128=h128, h64i2=h64i2, maskt=maskt)

    maps = []
    for core in range(8):
        b, j = core // 4, core % 4
        start = T * j
        xown = np.ascontiguousarray(x[b, start:start + T].reshape(NT, 128, DM))
        cosown = np.ascontiguousarray(cosf[start:start + T].reshape(NT, 128, HD)).astype(NPBF)
        sinown = np.ascontiguousarray(sinf[start:start + T].reshape(NT, 128, HD)).astype(NPBF)
        gm = np.zeros((128, 4), np.float32)
        gm[:, 0:j] = 1.0
        m = dict(shared)
        m.update(xown=xown, cosown=cosown, sinown=sinown, gmask=np.ascontiguousarray(gm))
        maps.append(m)
    return maps


def kernel(L0_attn_x, L0_attn_Wq, L0_attn_Wk, L0_attn_Wv, L0_attn_Wo,
           L0_ffn_W1, L0_ffn_W2, L0_ffn_W3):
    maps = make_inputs(L0_attn_x, L0_attn_Wq, L0_attn_Wk, L0_attn_Wv, L0_attn_Wo,
                       L0_ffn_W1, L0_ffn_W2, L0_ffn_W3)
    nc = build_nc()
    res = run_bass_kernel_spmd(nc, maps, core_ids=list(range(8)))
    outs = [r["out"].reshape(T, DM) for r in res.results]
    full = np.concatenate([np.stack(outs[0:4]).reshape(S, DM)[None],
                           np.stack(outs[4:8]).reshape(S, DM)[None]], axis=0)
    return full.astype(np.float32)
